# version 10
# baseline (speedup 1.0000x reference)
import numpy as np
import concourse.bass as bass
import concourse.mybir as mybir
from concourse.bass_utils import run_bass_kernel_spmd
from contextlib import ExitStack

F32 = mybir.dt.float32
BF16 = mybir.dt.bfloat16
AF = mybir.ActivationFunctionType
ALU = mybir.AluOpType
AX = mybir.AxisListType

D = 2048
NCH = 16
T = 512
NTT = 8
HT = 4
EHC = 64
DE = 8192
NW = float(DE * D)
SCALE = 128.0 ** -0.5
QTILE = 1024
TD = 256
NTD = 16
HSLOT = 32

_CLEANUP = "waits_only"
_ENTRY_CLEAR = True

PB_N = 76288
XN0 = 0
WAo = 32768
WBo = 36864
WCo = 40960
QBo = 45056
ERo = 47104
SILo = 49152
PHIKo = 50176
PKVo = 52224
PQOo = 54272
KVOo = 56320
HSTo = 58368
QB2o = 62464
WDo = 32768
HTTo = 65536
GDOo = 73728
CSTGo = 74240
GINo = 65536
WOo = 69632
SGTo = 71680


def ctx_off(c):
    return ((c + 8) % 16) * 4096


class KB:
    ENGS = ("pe", "act", "dve", "pool", "syncq")
    OWN = {"pe": "pe", "act": "act", "dve": "dve", "pool": "pool_c"}

    def __init__(self, nc, ctx):
        self.nc = nc
        self.ctx = ctx
        self.plan = {e: [] for e in self.ENGS}
        self.sems = {}
        self.cnt = {}
        self.wm = {e: {} for e in self.ENGS}

    def sem(self, name):
        if name not in self.sems:
            self.sems[name] = self.ctx.enter_context(self.nc.semaphore(name))
            self.cnt[name] = 0
        return self.sems[name]

    def wait(self, eng, tok):
        if tok is None:
            return
        s, v, force = tok
        if self.OWN.get(eng) == s and not force:
            return
        if self.wm[eng].get(s, 0) >= v:
            return
        self.wm[eng][s] = v
        h = self.sems[s]
        self.plan[eng].append(lambda e, h=h, v=v: e.wait_ge(h, v))

    def op(self, eng, fn, sem=None, n=1, waits=(), force=False):
        for t in waits:
            self.wait(eng, t)
        if sem is None:
            self.plan[eng].append(fn)
            return None
        self.sem(sem)
        self.cnt[sem] += n
        v = self.cnt[sem]
        h = self.sems[sem]
        self.plan[eng].append(lambda e, fn=fn, h=h, n=n: fn(e).then_inc(h, n))
        return (sem, v, force)

    def pe(self, fn, waits=(), force=False, inc=True):
        if inc:
            return self.op("pe", fn, "pe", 1, waits, force)
        self.op("pe", fn, None, 1, waits, force)
        return None

    def act(self, fn, waits=(), force=True):
        return self.op("act", fn, "act", 1, waits, force)

    def dve(self, fn, waits=(), force=True):
        return self.op("dve", fn, "dve", 1, waits, force)

    def pool(self, fn, waits=(), sem="pool_c", n=1, force=True):
        return self.op("pool", fn, sem, n, waits, force)

    def dma(self, queue, fn, sem, waits=()):
        return self.op(queue, fn, sem, 16, waits, False)


def build_nc():
    nc = bass.Bass("TRN2", target_bir_lowering=False)

    x_d = nc.dram_tensor("x", [D, NTT * T], F32, kind="ExternalInput").ap()
    wq_d = nc.dram_tensor("wq", [NCH, 128, D], F32, kind="ExternalInput").ap()
    wk_d = nc.dram_tensor("wk", [NCH, 128, D], F32, kind="ExternalInput").ap()
    wv_d = nc.dram_tensor("wv", [NCH, 128, D], F32, kind="ExternalInput").ap()
    wo_d = nc.dram_tensor("wo", [NCH, 128, D], F32, kind="ExternalInput").ap()
    wu_d = nc.dram_tensor("wu", [EHC, 128, D], F32, kind="ExternalInput").ap()
    wg_d = nc.dram_tensor("wg", [EHC, 128, D], F32, kind="ExternalInput").ap()
    wd_d = nc.dram_tensor("wd", [EHC, 128, D], F32, kind="ExternalInput").ap()
    wsl_d = nc.dram_tensor("wsl", [48, 128, QTILE], F32, kind="ExternalInput").ap()
    bq_d = nc.dram_tensor("bq", [128, NCH], F32, kind="ExternalInput").ap()
    bk_d = nc.dram_tensor("bk", [128, NCH], F32, kind="ExternalInput").ap()
    bv_d = nc.dram_tensor("bv", [128, NCH], F32, kind="ExternalInput").ap()
    bo_d = nc.dram_tensor("bo", [128, NCH], F32, kind="ExternalInput").ap()
    cst_d = nc.dram_tensor("cst", [128, 8], F32, kind="ExternalInput").ap()
    sel_d = nc.dram_tensor("sel", [128, NCH], F32, kind="ExternalInput").ap()

    y_d = nc.dram_tensor("y", [NCH, 128, NTT * T], F32, kind="ExternalOutput").ap()

    wuq = nc.dram_tensor("wuq", [EHC, 128, D], BF16)
    wgq = nc.dram_tensor("wgq", [EHC, 128, D], BF16)
    wdq = nc.dram_tensor("wdq", [EHC, 128, D], BF16)
    phiq_s = nc.dram_tensor("phiq_s", [NCH, 128, NTT * T], BF16)
    kv_s = nc.dram_tensor("kv_s", [NCH, 128, NTT * T], BF16)
    h_s = nc.dram_tensor("h_s", [EHC, 128, NTT * T], BF16)
    g_s = nc.dram_tensor("g_s", [NCH, 128, NTT * T], BF16)
    tin = nc.dram_tensor("tin", [NCH, 128], F32)
    tout = nc.dram_tensor("tout", [8 * NCH, 128], F32)
    rin = nc.dram_tensor("rin", [1, 8], F32)
    rgout = nc.dram_tensor("rgout", [8, 8], F32)

    with ExitStack() as ctx:
        def sb(name, shape, dt):
            return ctx.enter_context(nc.sbuf_tensor(name, shape, dt))

        x_t = sb("x_t", [128, NCH * T], F32)
        qin = [sb(f"qin{i}", [128, QTILE], F32) for i in range(2)]
        kvp0 = sb("kvp0", [128, T], F32)
        kvp = [kvp0, kvp0]
        tmp2 = [sb(f"tmp2_{i}", [128, T], F32) for i in range(2)]
        rowb = sb("rowb", [1, 2048], F32)
        srow_g = sb("srow_g", [1, 8], F32)
        rrow = sb("rrow", [1, 8], F32)
        srow2 = sb("srow2", [1, 8], F32)
        effrow = sb("effrow", [1, 8], F32)
        ones_col = sb("ones_col", [128, 1], F32)
        ones_row = sb("ones_row", [1, 128], F32)
        carry = sb("carry", [128, NCH], F32)
        cc_sc = sb("cc_sc", [128, NCH], F32)
        invc = sb("invc", [128, 4], F32)
        sc3 = sb("sc3", [128, 4], F32)
        inv1 = [sb(f"inv1_{i}", [128, 1], F32) for i in range(3)]
        effc = sb("effc", [128, 4], F32)
        redcol = sb("redcol", [128, 48], F32)
        redm = sb("redm", [128, 4], F32)
        g8t = sb("g8t", [8, 8], F32)
        bqc = sb("bqc", [128, NCH], F32)
        bkc = sb("bkc", [128, NCH], F32)
        bvc = sb("bvc", [128, NCH], F32)
        boc = sb("boc", [128, NCH], F32)
        cstc = sb("cstc", [128, 8], F32)
        selt = sb("selt", [128, NCH], F32)
        allg = sb("allg", [128, 128], F32)

        bp = sb("bp", [128, PB_N], BF16)
        ones_sc = sb("ones_sc", [128, T], BF16)

        pb = [ctx.enter_context(nc.psum_tensor(f"pb{i}", [128, T], F32))
              for i in range(8)]

        kb = KB(nc, ctx)

        def bpv(off, n):
            return bp[:, off:off + n]

        def xn_v(c, tl):
            return bpv(XN0 + c * 2048 + tl * T, T)

        def x_c(c):
            return x_t[:, c * T:(c + 1) * T]

        t_ones_sc = kb.dve(lambda e: e.memset(ones_sc[:, :], 1.0))
        t_ones_col = kb.dve(lambda e: e.memset(ones_col[:, :], 1.0))
        t_ones_row = kb.dve(lambda e: e.memset(ones_row[:, :], 1.0))
        t_carry0 = kb.dve(lambda e: e.memset(carry[:, :], 0.0))
        t_srow0 = kb.dve(lambda e: e.memset(srow_g[:, :], 0.0))

        kb.dma("pool", lambda e: e.dma_start(out=cstc[:, :], in_=cst_d), "cld")
        kb.dma("pool", lambda e: e.dma_start(out=bqc[:, :], in_=bq_d), "cld")
        kb.dma("pool", lambda e: e.dma_start(out=bkc[:, :], in_=bk_d), "cld")
        kb.dma("pool", lambda e: e.dma_start(out=bvc[:, :], in_=bv_d), "cld")
        kb.dma("pool", lambda e: e.dma_start(out=boc[:, :], in_=bo_d), "cld")
        tok_cld = kb.dma("pool", lambda e: e.dma_start(out=selt[:, :],
                                                       in_=sel_d), "cld")

        qin_war = [None, None]
        t_red = []

        def scale_load(i):
            sl = i % 2
            return kb.dma("act", lambda e, i=i, sl=sl: e.dma_start(
                out=qin[sl][:, :], in_=wsl_d[i]), f"qld{sl}",
                waits=[qin_war[sl]])

        t_sl = [None] * 48
        t_sl[0] = scale_load(0)
        t_sl[1] = scale_load(1)
        for i in range(48):
            tr = kb.dve(lambda e, i=i, sl=i % 2: e.tensor_reduce(
                redcol[:, i:i + 1], qin[sl][:, :], AX.X, ALU.add,
                apply_absolute_value=True), waits=[t_sl[i]])
            qin_war[i % 2] = tr
            t_red.append(tr)
            if i + 2 < 48:
                t_sl[i + 2] = scale_load(i + 2)
        t_redm = []
        for m in range(3):
            t_redm.append(kb.dve(lambda e, m=m: e.tensor_reduce(
                redm[:, m:m + 1], redcol[:, 16 * m:16 * (m + 1)], AX.X,
                ALU.add), waits=t_red[16 * m:16 * (m + 1)]))
        t_smm = [None] * 3
        for m in range(3):
            t_smm[m] = kb.pe(lambda e, m=m: e.matmul(
                pb[m][0:1, 0:1], redm[:, m:m + 1], ones_col[:, 0:1],
                start=True, stop=True), waits=[t_redm[m], t_ones_col])
        t_srow = None
        for m in range(3):
            t_srow = kb.act(lambda e, m=m: e.copy(
                srow_g[0:1, m:m + 1], pb[m][0:1, 0:1]),
                waits=[t_smm[m], t_srow0])
        t_rin = kb.dma("syncq", lambda e: e.dma_start(out=rin.ap(),
                                                      in_=srow_g[:, :]),
                       "srin", waits=[t_srow])
        t_cc1 = kb.pool(lambda e: e.collective_compute(
            "AllGather", ALU.bypass, replica_groups=[list(range(8))],
            ins=[rin.ap().opt()], outs=[rgout.ap().opt()]), waits=[t_rin],
            sem="cc")
        t_g8 = kb.dma("pool", lambda e: e.dma_start(out=g8t[:, :],
                                                    in_=rgout.ap()),
                      "sg8", waits=[t_cc1])
        t_gsum = kb.pe(lambda e: e.matmul(
            pb[3][0:1, 0:8], ones_col[0:8, 0:1], g8t[0:8, :],
            start=True, stop=True), waits=[t_g8, t_ones_col])
        t_rr = kb.act(lambda e: e.copy(rrow[0:1, 0:8], pb[3][0:1, 0:8]),
                      waits=[t_gsum])
        t_sc2 = kb.dve(lambda e: e.tensor_scalar(
            srow2[:, :], rrow[:, :], 1.0 / NW, 1e-8, op0=ALU.mult,
            op1=ALU.add), waits=[t_rr])
        t_e0 = kb.dve(lambda e: e.tensor_scalar(
            effrow[0:1, 0:1], srow2[0:1, 0:1], 0.5, None, op0=ALU.mult),
            waits=[t_sc2])
        t_e1 = kb.dve(lambda e: e.tensor_tensor(
            effrow[0:1, 1:2], srow2[0:1, 1:2], srow2[0:1, 2:3], ALU.mult),
            waits=[t_sc2])
        t_e1 = kb.dve(lambda e: e.tensor_scalar(
            effrow[0:1, 1:2], effrow[0:1, 1:2], 0.25, None, op0=ALU.mult),
            waits=[t_e1])
        t_bsc = kb.pe(lambda e: e.matmul(
            pb[0][:, 0:3], ones_row[0:1, :], srow2[0:1, 0:3], start=True,
            stop=True), waits=[t_sc2, t_ones_row, t_srow])
        t_beff = kb.pe(lambda e: e.matmul(
            pb[1][:, 0:2], ones_row[0:1, :], effrow[0:1, 0:2], start=True,
            stop=True), waits=[t_e0, t_e1, t_ones_row])
        t_sc3 = kb.act(lambda e: e.copy(sc3[:, 0:3], pb[0][:, 0:3]),
                       waits=[t_bsc])
        t_invc = None
        for m in range(3):
            t_rm = kb.dve(lambda e, m=m: e.reciprocal(inv1[m][:, :],
                                                      sc3[:, m:m + 1]),
                          waits=[t_sc3])
            t_invc = kb.dve(lambda e, m=m: e.tensor_copy(invc[:, m:m + 1],
                                                         inv1[m][:, :]),
                            waits=[t_rm])
        t_effc = kb.act(lambda e: e.copy(effc[:, 0:2], pb[1][:, 0:2]),
                        waits=[t_beff])
        bank_free = {0: t_sc3, 1: t_effc, 2: t_srow, 3: t_rr,
                     4: None, 5: None, 6: None, 7: None}

        q_srcs = (wu_d, wg_d, wd_d)
        q_dsts = (wuq, wgq, wdq)
        qz_tok = {}
        q_jobs = []
        for e2 in range(EHC):
            q_jobs += [(0, 2 * e2), (0, 2 * e2 + 1),
                       (1, 2 * e2), (1, 2 * e2 + 1)]
        q_jobs += [(2, j) for j in range(2 * EHC)]
        q_state = {"i": 0, "qin_war": [t_red[46], t_red[47]],
                   "qb1_war": [None, None], "qb2_war": [None, None],
                   "li": 0, "loads": {}}

        def q_tile_src(m, t_i):
            d = q_srcs[m]
            return d[t_i // 2, (t_i % 2) * 64:(t_i % 2) * 64 + 64, :].rearrange(
                "r (x f) -> (r x) f", x=2)

        def q_tile_dst(m, t_i):
            d = q_dsts[m].ap()
            return d[t_i // 2, (t_i % 2) * 64:(t_i % 2) * 64 + 64, :].rearrange(
                "r (x f) -> (r x) f", x=2)

        def q_load(j):
            st_ = q_state
            m, t_i = q_jobs[j]
            sl = j % 2
            st_["loads"][j] = kb.dma(
                "act", lambda e, m=m, t_i=t_i, sl=sl: e.dma_start(
                    out=qin[sl][:, :], in_=q_tile_src(m, t_i)), f"qld{sl}",
                waits=[st_["qin_war"][sl]])

        def q_emit(n):
            st_ = q_state
            for _ in range(n):
                if st_["i"] >= len(q_jobs):
                    return
                j = st_["i"]
                while st_["li"] <= min(j + 1, len(q_jobs) - 1):
                    q_load(st_["li"])
                    st_["li"] += 1
                m, t_i = q_jobs[j]
                sl = j % 2
                qb1 = bpv(QBo + sl * QTILE, QTILE)
                qb2 = bpv(QB2o + sl * QTILE, QTILE)
                tl_ = st_["loads"].pop(j)
                tb1 = kb.act(lambda e, m=m, sl=sl, qb1=qb1: e.activation(
                    qb1, qin[sl][:, :], AF.Sign,
                    scale=invc[:, m:m + 1], bias=cstc[:, 2:3]),
                    waits=[tl_, t_invc, st_["qb1_war"][sl], tok_cld])
                tb2 = kb.act(lambda e, m=m, sl=sl, qb2=qb2: e.activation(
                    qb2, qin[sl][:, :], AF.Sign,
                    scale=invc[:, m:m + 1], bias=cstc[:, 3:4]),
                    waits=[tl_, t_invc, st_["qb2_war"][sl]])
                st_["qin_war"][sl] = tb2
                tq = kb.dve(lambda e, qb1=qb1, qb2=qb2: e.tensor_tensor(
                    qb1, qb1, qb2, ALU.add), waits=[tb1, tb2])
                st_["qb2_war"][sl] = tq
                tst = kb.dma("syncq", lambda e, m=m, t_i=t_i, qb1=qb1:
                             e.dma_start(out=q_tile_dst(m, t_i), in_=qb1),
                             f"qz{sl}", waits=[tq])
                st_["qb1_war"][sl] = tst
                qz_tok[(m, t_i)] = tst
                st_["i"] += 1

        def q_emit_until(n):
            while q_state["i"] < min(n, len(q_jobs)):
                q_emit(1)

        st = {
            "bank_free": bank_free,
            "x_free": None,
            "rowb_free": None,
            "tmp_free": None,
            "xn_tok": {},
            "xn_readers": [],
            "wA_war": [None, None],
            "wB_war": [None, None],
            "wC_war": [None, None],
            "er_war": [None] * 4,
            "sil_war": [None, None],
            "phik_war": [None] * HT,
            "pqo_war": [None] * 4,
            "kvo_war": [None] * 4,
            "hst_war": [None, None],
            "kvp_war": [None, None],
            "pkv_war": [None] * HT,
            "carry_last": t_carry0,
            "pq_tok": {},
            "kv_tok": {},
            "h_tok": {},
            "g_tok": {},
            "scan_last": None,
        }

        def x_load_tt(tt, waits):
            src = x_d[:, tt * T:(tt + 1) * T].rearrange("(c p) t -> p c t",
                                                        p=128)
            dst = x_t[:, :].rearrange("p (c t) -> p c t", t=T)
            return kb.dma("act", lambda e: e.dma_start(out=dst, in_=src),
                          "xld", waits=waits)

        def emit_ln(half, tl):
            xw = [st["x_free"]] if st["x_free"] else []
            tok_x = x_load_tt(half * HT + tl, xw)
            t_sum = None
            t_sq = [None] * NCH
            t_ssq = [None] * NCH
            for c in range(NCH):
                w0 = [tok_x, st["bank_free"][4]] if c == 0 else [tok_x]
                t_sum = kb.pe(lambda e, c=c: e.matmul(
                    pb[4][0:1, :], ones_col[:, 0:1], x_c(c),
                    start=(c == 0), stop=(c == NCH - 1)),
                    waits=w0 + [t_ones_col])
                t_sq[c] = kb.act(lambda e, c=c: e.square(
                    tmp2[c % 2][:, :], x_c(c)),
                    waits=[tok_x, t_ssq[c - 2] if c >= 2 else st["tmp_free"]])
                wb = [st["bank_free"][5]] if c == 0 else []
                t_ssq[c] = kb.pe(lambda e, c=c: e.matmul(
                    pb[5][0:1, :], ones_col[:, 0:1], tmp2[c % 2][:, :],
                    start=(c == 0), stop=(c == NCH - 1)),
                    waits=wb + [t_sq[c]])
            rw = [st["rowb_free"]] if st["rowb_free"] else []
            t_m = kb.act(lambda e: e.activation(
                rowb[0:1, 0:512], pb[4][0:1, :], AF.Copy,
                scale=cstc[0:1, 0:1]), waits=[t_sum, tok_cld] + rw)
            t_msq = kb.act(lambda e: e.activation(
                rowb[0:1, 512:1024], pb[5][0:1, :], AF.Copy,
                scale=cstc[0:1, 0:1]), waits=[t_ssq[NCH - 1]] + rw)
            t_m2 = kb.dve(lambda e: e.tensor_tensor(
                rowb[0:1, 1024:1536], rowb[0:1, 0:512], rowb[0:1, 0:512],
                ALU.mult), waits=[t_m] + rw)
            t_var = kb.dve(lambda e: e.tensor_tensor(
                rowb[0:1, 512:1024], rowb[0:1, 512:1024],
                rowb[0:1, 1024:1536], ALU.subtract), waits=[t_msq, t_m2])
            t_lnv = kb.act(lambda e: e.activation(
                rowb[0:1, 1536:2048], rowb[0:1, 512:1024], AF.Ln,
                bias=cstc[0:1, 1:2]), waits=[t_var])
            t_r = kb.act(lambda e: e.activation(
                rowb[0:1, 1536:2048], rowb[0:1, 1536:2048], AF.Exp,
                scale=cstc[0:1, 2:3]), waits=[t_lnv])
            t_mr = kb.dve(lambda e: e.tensor_tensor(
                rowb[0:1, 1024:1536], rowb[0:1, 0:512], rowb[0:1, 1536:2048],
                ALU.mult), waits=[t_r, t_m, t_m2])
            t_Rb = kb.pe(lambda e: e.matmul(
                pb[6][:, :], ones_row[0:1, :], rowb[0:1, 1536:2048],
                start=True, stop=True),
                waits=[t_r, t_ones_row, st["bank_free"][6]])
            t_Mb = kb.pe(lambda e: e.matmul(
                pb[7][:, :], ones_row[0:1, :], rowb[0:1, 1024:1536],
                start=True, stop=True), waits=[t_mr, st["bank_free"][7]])
            st["rowb_free"] = t_Mb
            xn_war = list(st["xn_readers"])
            t_t1 = [None] * NCH
            t_xn = [None] * NCH
            for c in range(NCH):
                t_t1[c] = kb.dve(lambda e, c=c: e.tensor_tensor(
                    tmp2[c % 2][:, :], x_c(c), pb[6][:, :], ALU.mult),
                    waits=[t_Rb, tok_x, t_ssq[NCH - 1],
                           t_xn[c - 2] if c >= 2 else None])
                t_xn[c] = kb.dve(lambda e, c=c, tl=tl: e.tensor_tensor(
                    xn_v(c, tl), tmp2[c % 2][:, :], pb[7][:, :],
                    ALU.subtract),
                    waits=[t_Mb, t_t1[c]] + (xn_war if c < 2 else []))
                st["xn_tok"][(half, c, tl)] = t_xn[c]
            st["tmp_free"] = t_xn[NCH - 1]
            st["x_free"] = t_t1[NCH - 1]
            st["bank_free"][4] = t_m
            st["bank_free"][5] = t_msq
            st["bank_free"][6] = t_t1[NCH - 1]
            st["bank_free"][7] = t_xn[NCH - 1]

        def mm_group(bank, wview, rhs_of_c, extra_waits, nk=NCH):
            tokl = None
            for c in range(nk):
                w = list(extra_waits) if c == 0 else []
                tokl = kb.pe(lambda e, c=c: e.matmul(
                    pb[bank][:, :], wview[:, c * 128:(c + 1) * 128],
                    rhs_of_c(c), start=(c == 0), stop=(c == nk - 1)),
                    waits=w, inc=(c == nk - 1))
            return tokl

        def emit_qkv(half, qpace, qpace_kv=None):
            if qpace_kv is None:
                qpace_kv = qpace
            xn_tok = st["xn_tok"]

            def wA_load(wdram, oc, sl):
                return kb.dma("pool", lambda e, oc=oc, sl=sl, wdram=wdram:
                              e.dma_start(out=bpv(WAo + sl * D, D),
                                          in_=wdram[oc]),
                              f"wA{sl}", waits=[st["wA_war"][sl]])

            def phi_epilogue(oc, tl, bank, bias, dst, dst_war_key, war_list):
                t_e = kb.act(lambda e, oc=oc, tl=tl, bank=bank, bias=bias:
                             e.activation(
                                 bpv(ERo + tl * T, T), pb[bank][:, :],
                                 AF.Exp, bias=bias[:, oc:oc + 1]),
                             waits=[war_list[0], st["er_war"][tl], tok_cld])
                t_r2 = kb.act(lambda e, oc=oc, tl=tl, bank=bank, bias=bias:
                              e.activation(
                                  bpv(SILo + (tl % 2) * T, T),
                                  pb[bank][:, :], AF.Relu,
                                  bias=bias[:, oc:oc + 1]),
                              waits=[war_list[0], st["sil_war"][tl % 2]])
                st["bank_free"][bank] = t_r2
                t_phi = kb.dve(lambda e, tl=tl, dst=dst:
                               e.scalar_tensor_tensor(
                                   dst, bpv(ERo + tl * T, T), 1.0,
                                   bpv(SILo + (tl % 2) * T, T),
                                   op0=ALU.min, op1=ALU.add),
                               waits=[t_e, t_r2, dst_war_key])
                st["er_war"][tl] = t_phi
                st["sil_war"][tl % 2] = t_phi
                return t_phi

            t_w = {0: wA_load(wq_d, 0, 0), 1: wA_load(wq_d, 1, 1)}
            for oc in range(NCH):
                sl = oc % 2
                grp_last = None
                for tl in range(HT):
                    bank = tl
                    ew = [t_w[oc], xn_tok[(half, 15, tl)],
                          st["bank_free"][bank]]
                    grp = mm_group(bank, bpv(WAo + sl * D, D),
                                   lambda c, tl=tl: xn_v(c, tl), ew)
                    st["bank_free"][bank] = grp
                    t_phi = phi_epilogue(oc, tl, bank, bqc,
                                         bpv(PQOo + tl * T, T),
                                         st["pqo_war"][tl], [grp])
                    t_pst = kb.dma("syncq", lambda e, oc=oc, half=half, tl=tl:
                                   e.dma_start(
                                       out=phiq_s.ap()[oc][
                                           :, (half * HT + tl) * T:
                                           (half * HT + tl + 1) * T],
                                       in_=bpv(PQOo + tl * T, T)),
                                   f"spq{tl}", waits=[t_phi])
                    st["pqo_war"][tl] = t_pst
                    st["pq_tok"][oc] = t_pst
                    grp_last = grp
                st["wA_war"][sl] = grp_last
                if oc + 2 < NCH:
                    t_w[oc + 2] = wA_load(wq_d, oc + 2, sl)
                if oc >= 2:
                    q_emit(qpace)

            t_wk = {0: wA_load(wk_d, 0, 0)}
            t_wv = {0: wA_load(wv_d, 0, 1)}
            for oc in range(NCH):
                klast = None
                for tl in range(HT):
                    bank = tl
                    ew = [t_wk[oc], xn_tok[(half, 15, tl)],
                          st["bank_free"][bank]]
                    grp = mm_group(bank, bpv(WAo + 0 * D, D),
                                   lambda c, tl=tl: xn_v(c, tl), ew)
                    st["bank_free"][bank] = grp
                    phi_epilogue(oc, tl, bank, bkc, bpv(PHIKo + tl * T, T),
                                 st["phik_war"][tl], [grp])
                    st["phik_war"][tl] = st["er_war"][tl]
                    klast = grp
                st["wA_war"][0] = klast
                if oc + 1 < NCH:
                    t_wk[oc + 1] = wA_load(wk_d, oc + 1, 0)
                vlast = None
                for tl in range(HT):
                    bank = tl
                    ew = [t_wv[oc], st["bank_free"][bank]]
                    grp = mm_group(bank, bpv(WAo + 1 * D, D),
                                   lambda c, tl=tl: xn_v(c, tl), ew)
                    t_pkv = kb.dve(lambda e, oc=oc, tl=tl, bank=bank:
                                   e.scalar_tensor_tensor(
                                       bpv(PKVo + tl * T, T),
                                       pb[bank][:, :], bvc[:, oc:oc + 1],
                                       bpv(PHIKo + tl * T, T),
                                       op0=ALU.add, op1=ALU.mult),
                                   waits=[grp, st["phik_war"][tl],
                                          st["pkv_war"][tl], tok_cld])
                    st["bank_free"][bank] = t_pkv
                    st["phik_war"][tl] = t_pkv
                    vlast = grp
                    ssl = tl % 2
                    t_scan = kb.dve(lambda e, oc=oc, tl=tl, ssl=ssl:
                                    e.tensor_tensor_scan(
                                        kvp[ssl][:, :], ones_sc[:, :],
                                        bpv(PKVo + tl * T, T),
                                        carry[:, oc:oc + 1],
                                        op0=ALU.mult, op1=ALU.add),
                                    waits=[t_pkv, t_ones_sc,
                                           st["carry_last"],
                                           st["kvp_war"][ssl]], force=True)
                    t_carry = kb.dve(lambda e, oc=oc, ssl=ssl: e.tensor_copy(
                        carry[:, oc:oc + 1], kvp[ssl][:, T - 1:T]),
                        waits=[t_scan], force=True)
                    st["carry_last"] = t_carry
                    st["pkv_war"][tl] = t_scan
                    t_kvo = kb.dve(lambda e, ssl=ssl, tl=tl: e.tensor_scalar(
                        bpv(KVOo + tl * T, T), kvp[ssl][:, :], SCALE, None,
                        op0=ALU.mult), waits=[t_scan, st["kvo_war"][tl]])
                    st["kvp_war"][ssl] = t_kvo
                    t_kst = kb.dma("syncq", lambda e, oc=oc, half=half, tl=tl:
                                   e.dma_start(
                                       out=kv_s.ap()[oc][
                                           :, (half * HT + tl) * T:
                                           (half * HT + tl + 1) * T],
                                       in_=bpv(KVOo + tl * T, T)),
                                   f"skv{tl}", waits=[t_kvo])
                    st["kvo_war"][tl] = t_kst
                    st["kv_tok"][oc] = t_kst
                st["wA_war"][1] = vlast
                if oc + 1 < NCH:
                    t_wv[oc + 1] = wA_load(wv_d, oc + 1, 1)
                st["xn_readers"] = [vlast]
                q_emit(qpace_kv)
            st["scan_last"] = st["carry_last"]

        def emit_ug(half, qpace, wqueue="pool"):
            xn_tok = st["xn_tok"]

            def wB_load(eh):
                sl = eh % 2
                return kb.dma(wqueue, lambda e, eh=eh, sl=sl: e.dma_start(
                    out=bpv(WBo + sl * D, D), in_=wuq.ap()[eh]), f"wB{sl}",
                    waits=[st["wB_war"][sl], qz_tok.get((0, 2 * eh)),
                           qz_tok.get((0, 2 * eh + 1))])

            def wC_load(eh):
                sl = eh % 2
                return kb.dma(wqueue, lambda e, eh=eh, sl=sl: e.dma_start(
                    out=bpv(WCo + sl * D, D), in_=wgq.ap()[eh]), f"wC{sl}",
                    waits=[st["wC_war"][sl], qz_tok.get((1, 2 * eh)),
                           qz_tok.get((1, 2 * eh + 1))])

            t_wu = {0: wB_load(0), 1: wB_load(1)}
            t_wg = {0: wC_load(0), 1: wC_load(1)}
            for eh in range(EHC):
                sl = eh % 2
                hsl = eh % 2
                up_last = gate_last = None
                t_h = None
                for tl in range(HT):
                    ub = tl % 2
                    gb = 2 + tl % 2
                    ew = [t_wu[eh], xn_tok[(half, 15, tl)],
                          st["bank_free"][ub]]
                    up = mm_group(ub, bpv(WBo + sl * D, D),
                                  lambda c, tl=tl: xn_v(c, tl), ew)
                    t_sil = kb.act(lambda e, ub=ub, tl=tl: e.activation(
                        bpv(SILo + (tl % 2) * T, T), pb[ub][:, :], AF.Silu,
                        scale=effc[:, 0:1]),
                        waits=[up, t_effc, st["sil_war"][tl % 2]])
                    st["bank_free"][ub] = t_sil
                    ewg = [t_wg[eh], st["bank_free"][gb]]
                    gate = mm_group(gb, bpv(WCo + sl * D, D),
                                    lambda c, tl=tl: xn_v(c, tl), ewg)
                    t_h = kb.dve(lambda e, gb=gb, tl=tl, hsl=hsl:
                                 e.tensor_tensor(
                                     bpv(HSTo + hsl * 2048 + tl * T, T),
                                     bpv(SILo + (tl % 2) * T, T),
                                     pb[gb][:, :], ALU.mult),
                                 waits=[t_sil, gate, st["hst_war"][hsl]])
                    st["sil_war"][tl % 2] = t_h
                    st["bank_free"][gb] = t_h
                    up_last, gate_last = up, gate
                t_hst = kb.dma("syncq", lambda e, eh=eh, half=half, hsl=hsl:
                               e.dma_start(
                                   out=h_s.ap()[eh][:, half * HT * T:
                                                    (half + 1) * HT * T],
                                   in_=bpv(HSTo + hsl * 2048, 2048)),
                               f"sh{hsl}", waits=[t_h])
                st["hst_war"][hsl] = t_hst
                st["h_tok"][(eh, half)] = t_hst
                st["wB_war"][sl] = up_last
                st["wC_war"][sl] = gate_last
                if eh + 2 < EHC:
                    t_wu[eh + 2] = wB_load(eh + 2)
                    t_wg[eh + 2] = wC_load(eh + 2)
                st["xn_readers"] = [gate_last]
                q_emit(qpace)
                yield eh

        for tl in range(HT):
            emit_ln(0, tl)
        emit_qkv(0, qpace=3, qpace_kv=4)
        for _ in emit_ug(0, qpace=3):
            pass

        for tl in range(HT):
            emit_ln(1, tl)
        emit_qkv(1, qpace=0)

        t_tin = None
        for c in range(NCH):
            t_tin = kb.dma("syncq", lambda e, c=c: e.dma_start(
                out=tin.ap()[c:c + 1, :].rearrange("a b -> b a"),
                in_=carry[:, c:c + 1]), "stin", waits=[st["scan_last"]])
        t_cc2 = kb.pool(lambda e: e.collective_compute(
            "AllGather", ALU.bypass, replica_groups=[list(range(8))],
            ins=[tin.ap().opt()], outs=[tout.ap().opt()]),
            waits=[t_tin, t_cc1], sem="cc")
        t_allg = kb.dma("pool", lambda e: e.dma_start(
            out=allg[:, :], in_=tout.ap()), "sg8", waits=[t_cc2])

        t_ccsc = None
        for eh in emit_ug(1, qpace=2, wqueue="act"):
            if eh == 32:
                t_selmm = kb.pe(lambda e: e.matmul(
                    pb[5][:, 0:NCH], allg[:, :], selt[:, :], start=True,
                    stop=True), waits=[t_allg, tok_cld, st["bank_free"][5]])
                t_ccsc = kb.act(lambda e: e.copy(cc_sc[:, :],
                                                 pb[5][:, 0:NCH]),
                                waits=[t_selmm])
                st["bank_free"][5] = t_ccsc

        q_emit_until(len(q_jobs))

        region_free = ([st["wB_war"][0], st["wB_war"][1], st["wC_war"][0],
                        st["wC_war"][1], st["hst_war"][0], st["hst_war"][1],
                        q_state["qb1_war"][0], q_state["qb1_war"][1],
                        q_state["qb2_war"][0], q_state["qb2_war"][1],
                        st["sil_war"][0], st["sil_war"][1]]
                       + list(st["pkv_war"])
                       + list(st["er_war"]) + list(st["phik_war"])
                       + list(st["pqo_war"]) + list(st["kvo_war"]))
        region_free = [t for t in region_free if t is not None]
        xn_free = list(st["xn_readers"])

        def wd_tile(ocl, j):
            return bpv(WDo + (ocl * 8 + j) * QTILE, QTILE)

        dn = {
            "htt_war": [None] * HSLOT,
            "wd_war": {},
            "gdo_war": [None] * 2,
            "ctx_tok": {},
            "ctx_kv_war": [None, None],
            "cstg_war": [None, None],
            "fill_tok": {},
        }

        def emit_ctx(c, extra_waits=(), kvq="syncq"):
            co = ctx_off(c)
            t_kv = kb.dma(kvq, lambda e, c=c, co=co: e.dma_start(
                out=bpv(co, 4096), in_=kv_s.ap()[c]), f"ckv{c % 2}",
                waits=[st["kv_tok"][c], dn["ctx_kv_war"][c % 2]]
                + list(extra_waits))
            t_ctx = None
            for hh in range(4):
                qsl = hh % 2
                t_pq = kb.dma("syncq", lambda e, c=c, hh=hh, qsl=qsl:
                              e.dma_start(
                                  out=bpv(CSTGo + qsl * 1024, 1024),
                                  in_=phiq_s.ap()[c][:, hh * 1024:
                                                     (hh + 1) * 1024]),
                              f"cpq{qsl}", waits=[st["pq_tok"][c],
                                                  dn["cstg_war"][qsl]])
                t_ctx = kb.dve(lambda e, c=c, co=co, hh=hh, qsl=qsl:
                               e.scalar_tensor_tensor(
                                   bpv(co + hh * 1024, 1024),
                                   bpv(co + hh * 1024, 1024),
                                   cc_sc[:, c:c + 1],
                                   bpv(CSTGo + qsl * 1024, 1024),
                                   op0=ALU.add, op1=ALU.mult),
                               waits=[t_kv, t_pq, t_ccsc])
                dn["cstg_war"][qsl] = t_ctx
            dn["ctx_kv_war"][c % 2] = t_ctx
            dn["ctx_tok"][c] = t_ctx

        ctx_queue = list(range(8, NCH))
        h_queues = ["act", "pool"]

        def h_load(ec, td, war):
            slot = ec % HSLOT
            return kb.dma(
                h_queues[ec % 2],
                lambda e, ec=ec, td=td, slot=slot: e.dma_start(
                    out=bpv(HTTo + slot * TD, TD),
                    in_=h_s.ap()[ec][:, td * TD:(td + 1) * TD]),
                f"ht{slot}", waits=[st["h_tok"][(ec, td // 8)], war])

        for oq in range(4):
            fill_prev = {}
            nf = 0
            for j in range(8):
                for ocl in range(4):
                    oc = oq * 4 + ocl
                    k = oc * 8 + j
                    fsl = nf % 8
                    w = [qz_tok[(2, k)], fill_prev.get(fsl),
                         dn["wd_war"].get((ocl, j))]
                    if oq == 0 and nf == 0:
                        w += region_free + xn_free
                    tf = kb.dma("syncq",
                                lambda e, ocl=ocl, j=j, k=k: e.dma_start(
                                    out=wd_tile(ocl, j),
                                    in_=q_tile_dst(2, k)),
                                f"wdf{fsl}", waits=w)
                    dn["fill_tok"][(oq, ocl, j)] = tf
                    fill_prev[fsl] = tf
                    nf += 1
            for td in range(NTD):
                t_hld = {}
                for ec in range(HSLOT):
                    t_hld[ec] = h_load(ec, td, dn["htt_war"][ec % HSLOT])
                for ec in range(EHC):
                    slot = ec % HSLOT
                    j, e8 = ec // 8, ec % 8
                    last8 = None
                    for ocl in range(4):
                        bank = (td % 2) * 4 + ocl
                        w = [t_hld[ec]]
                        if ec == 0:
                            w += [dn["fill_tok"][(oq, ocl, 0)],
                                  st["bank_free"][bank]]
                        elif e8 == 0:
                            w.append(dn["fill_tok"][(oq, ocl, j)])
                        last8 = kb.pe(
                            lambda e, ocl=ocl, j=j, e8=e8, slot=slot,
                            bank=bank, ec=ec: e.matmul(
                                pb[bank][:, 0:TD],
                                wd_tile(ocl, j)[:, e8 * 128:(e8 + 1) * 128],
                                bpv(HTTo + slot * TD, TD),
                                start=(ec == 0), stop=(ec == EHC - 1)),
                            waits=w, inc=(ocl == 3))
                    for ocl in range(4):
                        dn["wd_war"][(ocl, j)] = last8
                    dn["htt_war"][slot] = last8
                    if ec + HSLOT < EHC:
                        t_hld[ec + HSLOT] = h_load(ec + HSLOT, td, last8)
                for ocl in range(4):
                    oc = oq * 4 + ocl
                    bank = (td % 2) * 4 + ocl
                    gsl = ocl % 2
                    t_g = kb.act(lambda e, bank=bank, gsl=gsl: e.activation(
                        bpv(GDOo + gsl * TD, TD), pb[bank][:, 0:TD],
                        AF.Silu, scale=effc[:, 1:2]),
                        waits=[dn["wd_war"][(ocl, 7)], t_effc,
                               dn["gdo_war"][gsl]])
                    st["bank_free"][bank] = t_g
                    t_gst = kb.dma("syncq", lambda e, oc=oc, td=td, gsl=gsl:
                                   e.dma_start(
                                       out=g_s.ap()[oc][:, td * TD:
                                                        (td + 1) * TD],
                                       in_=bpv(GDOo + gsl * TD, TD)),
                                   f"sgd{gsl}", waits=[t_g])
                    dn["gdo_war"][gsl] = t_gst
                    st["g_tok"][oc] = t_gst
                if ctx_queue and td % 3 == 2:
                    c = ctx_queue.pop(0)
                    emit_ctx(c, extra_waits=(xn_free if c == 8 else ()))
        while ctx_queue:
            c = ctx_queue.pop(0)
            emit_ctx(c, extra_waits=(xn_free if c == 8 else ()))

        down_done = [dn["wd_war"][(ocl, 7)] for ocl in range(4)]
        htt_done = [t for t in dn["htt_war"] if t is not None]
        for c in range(8):
            emit_ctx(c, extra_waits=(down_done if c < 2 else ()), kvq="act")

        fin = {
            "wo_war": None,
            "x_war": [None, None],
            "gin_war": None,
            "sgt_war": [None, None],
            "y_tok": [None, None],
        }
        for oc in range(NCH):
            sl = oc % 2
            t_wo = kb.dma("pool", lambda e, oc=oc: e.dma_start(
                out=bpv(WOo, D), in_=wo_d[oc]), "wo0",
                waits=[fin["wo_war"]] + (htt_done if oc == 0 else []))
            t_x = kb.dma("syncq", lambda e, oc=oc, sl=sl: e.dma_start(
                out=x_t[:, sl * 4096:(sl + 1) * 4096],
                in_=x_d[oc * 128:(oc + 1) * 128, :]), f"fx{sl}",
                waits=[fin["y_tok"][sl], fin["x_war"][sl]]
                + ([st["x_free"], st["tmp_free"]] if oc < 2 else []))
            t_gin = kb.dma("act", lambda e, oc=oc: e.dma_start(
                out=bpv(GINo, 4096), in_=g_s.ap()[oc]), "fg",
                waits=[st["g_tok"][oc], fin["gin_war"]]
                + (htt_done if oc == 0 else []))
            t_sgt = None
            t_out = [None] * NTT
            c_order = list(range(8, NCH)) + list(range(8))
            for tt in range(NTT):
                bank = 4 + tt % 4
                grp = None
                for i, c in enumerate(c_order):
                    w = [t_wo, st["bank_free"][bank]] if i == 0 else []
                    if oc == 0:
                        w.append(dn["ctx_tok"][c])
                    grp = kb.pe(lambda e, c=c, i=i, bank=bank, tt=tt:
                                e.matmul(
                                    pb[bank][:, :],
                                    bpv(WOo + c * 128, 128),
                                    bpv(ctx_off(c) + tt * T, T),
                                    start=(i == 0), stop=(i == NCH - 1)),
                                waits=w, inc=(i == NCH - 1))
                ssl = tt % 2
                sgw = [grp, t_gin, fin["sgt_war"][ssl]]
                if oc == 0 and tt == 0:
                    sgw += htt_done
                t_sgt = kb.dve(lambda e, oc=oc, tt=tt, ssl=ssl, bank=bank:
                               e.scalar_tensor_tensor(
                                   bpv(SGTo + ssl * T, T), pb[bank][:, :],
                                   boc[:, oc:oc + 1],
                                   bpv(GINo + tt * T, T),
                                   op0=ALU.add, op1=ALU.mult),
                               waits=sgw)
                st["bank_free"][bank] = t_sgt
                t_out[tt] = kb.dve(lambda e, tt=tt, sl=sl, ssl=ssl:
                                   e.tensor_tensor(
                                       x_t[:, sl * 4096 + tt * T:
                                           sl * 4096 + (tt + 1) * T],
                                       x_t[:, sl * 4096 + tt * T:
                                           sl * 4096 + (tt + 1) * T],
                                       bpv(SGTo + ssl * T, T), ALU.add),
                                   waits=[t_sgt, t_x])
                fin["sgt_war"][ssl] = t_out[tt]
            fin["wo_war"] = grp
            fin["gin_war"] = t_sgt
            fin["x_war"][sl] = t_out[NTT - 1]
            t_y = kb.dma("syncq", lambda e, oc=oc, sl=sl: e.dma_start(
                out=y_d[oc], in_=x_t[:, sl * 4096:(sl + 1) * 4096]),
                f"syo{sl}", waits=[t_out[NTT - 1]])
            fin["y_tok"][sl] = t_y

        kb.cnt["cc"] += 1

        def _final_sync(e):
            e.wait_ge(kb.sems["syo0"], kb.cnt["syo0"])
            e.wait_ge(kb.sems["syo1"], kb.cnt["syo1"])
            e.nop().then_inc(kb.sems["cc"], 1)
        kb.op("syncq", _final_sync)

        def _cleanup(e):
            for sname in kb.sems:
                e.wait_ge(kb.sems[sname], kb.cnt[sname])
            if _CLEANUP != "waits_only":
                for sname in kb.sems:
                    e.sem_clear(kb.sems[sname])
        if _CLEANUP:
            kb.op("pool", _cleanup)

        if _ENTRY_CLEAR:
            nums = sorted(h.num for h in kb.sems.values())
            nc.all_engine_barrier()
            nc.gpsimd.dma_reset(range(nums[0], nums[-1] + 1))
            nc.gpsimd.sem_clear(range(nums[0], nums[-1] + 1))
            nc.all_engine_barrier()
        with nc.Block() as block:
            @block.tensor
            def _(eng):
                for f in kb.plan["pe"]:
                    f(eng)

            @block.scalar
            def _(eng):
                for f in kb.plan["act"]:
                    f(eng)

            @block.vector
            def _(eng):
                for f in kb.plan["dve"]:
                    f(eng)

            @block.gpsimd
            def _(eng):
                for f in kb.plan["pool"]:
                    f(eng)

            @block.sync
            def _(eng):
                for f in kb.plan["syncq"]:
                    f(eng)

    return nc


def _prep_shared(Wq, Wk, Wv, Wo, Wu, Wg, Wd, bq, bk, bv, bo):
    def attn_stream(W):
        WT = np.ascontiguousarray(np.asarray(W, np.float32).T)
        return np.ascontiguousarray(
            WT.reshape(16, 128, 16, 128).transpose(2, 1, 0, 3)
        ).reshape(16, 128, 2048)

    def upgate_stream(W):
        WT = np.ascontiguousarray(np.asarray(W, np.float32).T)
        return np.ascontiguousarray(
            WT.reshape(16, 128, 64, 128).transpose(2, 1, 0, 3)
        ).reshape(64, 128, 2048)

    def down_stream(W):
        WT = np.ascontiguousarray(np.asarray(W, np.float32).T)
        A = WT.reshape(64, 128, 16, 128).transpose(2, 0, 1, 3)
        A = A.reshape(16, 8, 8, 128, 128).transpose(0, 1, 3, 2, 4)
        return np.ascontiguousarray(A).reshape(64, 128, 2048)

    def bcol(b):
        return np.ascontiguousarray(np.asarray(b, np.float32).reshape(16, 128).T)

    d = {
        "wq": attn_stream(Wq), "wk": attn_stream(Wk), "wv": attn_stream(Wv),
        "wo": attn_stream(Wo),
        "wu": upgate_stream(Wu), "wg": upgate_stream(Wg), "wd": down_stream(Wd),
        "bq": bcol(bq), "bk": bcol(bk), "bv": bcol(bv), "bo": bcol(bo),
        "cst": np.tile(np.array([[1.0 / 2048, 1e-5, -0.5, 0.5, 0, 0, 0, 0]],
                                np.float32), (128, 1)),
    }
    return d


def kernel(L0_attn_x, L0_attn_Wq, L0_attn_bq, L0_attn_Wk, L0_attn_bk,
           L0_attn_Wv, L0_attn_bv, L0_attn_Wo, L0_attn_bo,
           L0_ffn_W_up, L0_ffn_W_gate, L0_ffn_W_down):
    x = np.asarray(L0_attn_x, np.float32)
    shared = _prep_shared(L0_attn_Wq, L0_attn_Wk, L0_attn_Wv, L0_attn_Wo,
                          L0_ffn_W_up, L0_ffn_W_gate, L0_ffn_W_down,
                          L0_attn_bq, L0_attn_bk, L0_attn_bv, L0_attn_bo)

    wu_flat = shared["wu"].reshape(8192, 2048)
    wg_flat = shared["wg"].reshape(8192, 2048)
    wd_flat = shared["wd"].reshape(8192, 2048)

    in_maps = []
    for c in range(8):
        b, h = c // 2, c % 2
        x_fm = np.ascontiguousarray(x[b, h * 4096:(h + 1) * 4096, :].T)
        sel = np.zeros((128, 16), np.float32)
        if h == 1:
            r = c - 1
            for ch in range(16):
                sel[16 * r + ch, ch] = SCALE
        wsl = np.ascontiguousarray(np.stack([
            wu_flat[c * 1024:(c + 1) * 1024],
            wg_flat[c * 1024:(c + 1) * 1024],
            wd_flat[c * 1024:(c + 1) * 1024]])).reshape(48, 128, 1024)
        m = dict(shared)
        m["x"] = x_fm
        m["sel"] = sel
        m["wsl"] = wsl
        in_maps.append(m)

    nc = build_nc()
    res = run_bass_kernel_spmd(nc, in_maps, core_ids=list(range(8)))

    out = np.empty((4, 8192, 2048), np.float32)
    for c in range(8):
        b, h = c // 2, c % 2
        y = np.asarray(res.results[c]["y"])
        out_fm = y.reshape(2048, 4096)
        out[b, h * 4096:(h + 1) * 4096, :] = out_fm.T
    return out


# version 11
# speedup vs baseline: 1.0349x; 1.0349x over previous
import numpy as np
import concourse.bass as bass
import concourse.mybir as mybir
from concourse.bass_utils import run_bass_kernel_spmd
from contextlib import ExitStack

F32 = mybir.dt.float32
BF16 = mybir.dt.bfloat16
AF = mybir.ActivationFunctionType
ALU = mybir.AluOpType
AX = mybir.AxisListType

D = 2048
NCH = 16
T = 512
NTT = 8
HT = 4
EHC = 64
DE = 8192
NW = float(DE * D)
SCALE = 128.0 ** -0.5
QTILE = 1024
TD = 256
NTD = 16
HSLOT = 24

_CLEANUP = "waits_only"
_ENTRY_CLEAR = True

PB_N = 74752
XN0 = 0
WAo = 32768
WBo = 36864
WCo = 40960
QBo = 45056
ERo = 47104
SILo = 49152
PHIKo = 50176
PKVo = 52224
PQOo = 54272
KVOo = 56320
HSTo = 58368
QB2o = 62464
WDo = 32768
HTTo = 65536
GDOo = 71680
CSTGo = 72704
GINo = 65536
WOo = 69632
SGTo = 71680


def ctx_off(c):
    return ((c + 8) % 16) * 4096


class KB:
    ENGS = ("pe", "act", "dve", "pool", "syncq")
    OWN = {"pe": "pe", "act": "act", "dve": "dve", "pool": "pool_c"}

    def __init__(self, nc, ctx):
        self.nc = nc
        self.ctx = ctx
        self.plan = {e: [] for e in self.ENGS}
        self.sems = {}
        self.cnt = {}
        self.wm = {e: {} for e in self.ENGS}

    def sem(self, name):
        if name not in self.sems:
            self.sems[name] = self.ctx.enter_context(self.nc.semaphore(name))
            self.cnt[name] = 0
        return self.sems[name]

    def wait(self, eng, tok):
        if tok is None:
            return
        s, v, force = tok
        if self.OWN.get(eng) == s and not force:
            return
        if self.wm[eng].get(s, 0) >= v:
            return
        self.wm[eng][s] = v
        h = self.sems[s]
        self.plan[eng].append(lambda e, h=h, v=v: e.wait_ge(h, v))

    def op(self, eng, fn, sem=None, n=1, waits=(), force=False):
        for t in waits:
            self.wait(eng, t)
        if sem is None:
            self.plan[eng].append(fn)
            return None
        self.sem(sem)
        self.cnt[sem] += n
        v = self.cnt[sem]
        h = self.sems[sem]
        self.plan[eng].append(lambda e, fn=fn, h=h, n=n: fn(e).then_inc(h, n))
        return (sem, v, force)

    def pe(self, fn, waits=(), force=False, inc=True):
        if inc:
            return self.op("pe", fn, "pe", 1, waits, force)
        self.op("pe", fn, None, 1, waits, force)
        return None

    def act(self, fn, waits=(), force=True):
        return self.op("act", fn, "act", 1, waits, force)

    def dve(self, fn, waits=(), force=True):
        return self.op("dve", fn, "dve", 1, waits, force)

    def pool(self, fn, waits=(), sem="pool_c", n=1, force=True):
        return self.op("pool", fn, sem, n, waits, force)

    def dma(self, queue, fn, sem, waits=()):
        return self.op(queue, fn, sem, 16, waits, False)


def build_nc():
    nc = bass.Bass("TRN2", target_bir_lowering=False)

    x_d = nc.dram_tensor("x", [D, NTT * T], F32, kind="ExternalInput").ap()
    wq_d = nc.dram_tensor("wq", [NCH, 128, D], F32, kind="ExternalInput").ap()
    wk_d = nc.dram_tensor("wk", [NCH, 128, D], F32, kind="ExternalInput").ap()
    wv_d = nc.dram_tensor("wv", [NCH, 128, D], F32, kind="ExternalInput").ap()
    wo_d = nc.dram_tensor("wo", [NCH, 128, D], F32, kind="ExternalInput").ap()
    wu_d = nc.dram_tensor("wu", [EHC, 128, D], F32, kind="ExternalInput").ap()
    wg_d = nc.dram_tensor("wg", [EHC, 128, D], F32, kind="ExternalInput").ap()
    wd_d = nc.dram_tensor("wd", [EHC, 128, D], F32, kind="ExternalInput").ap()
    wsl_d = nc.dram_tensor("wsl", [48, 128, QTILE], F32, kind="ExternalInput").ap()
    bq_d = nc.dram_tensor("bq", [128, NCH], F32, kind="ExternalInput").ap()
    bk_d = nc.dram_tensor("bk", [128, NCH], F32, kind="ExternalInput").ap()
    bv_d = nc.dram_tensor("bv", [128, NCH], F32, kind="ExternalInput").ap()
    bo_d = nc.dram_tensor("bo", [128, NCH], F32, kind="ExternalInput").ap()
    cst_d = nc.dram_tensor("cst", [128, 8], F32, kind="ExternalInput").ap()
    sel_d = nc.dram_tensor("sel", [128, NCH], F32, kind="ExternalInput").ap()

    y_d = nc.dram_tensor("y", [NCH, 128, NTT * T], F32, kind="ExternalOutput").ap()

    wuq = nc.dram_tensor("wuq", [EHC, 128, D], BF16)
    wgq = nc.dram_tensor("wgq", [EHC, 128, D], BF16)
    wdq = nc.dram_tensor("wdq", [EHC, 128, D], BF16)
    phiq_s = nc.dram_tensor("phiq_s", [NCH, 128, NTT * T], BF16)
    kv_s = nc.dram_tensor("kv_s", [NCH, 128, NTT * T], BF16)
    h_s = nc.dram_tensor("h_s", [EHC, 128, NTT * T], BF16)
    g_s = nc.dram_tensor("g_s", [NCH, 128, NTT * T], BF16)
    tin = nc.dram_tensor("tin", [NCH, 128], F32)
    tout = nc.dram_tensor("tout", [8 * NCH, 128], F32)
    rin = nc.dram_tensor("rin", [1, 8], F32)
    rgout = nc.dram_tensor("rgout", [8, 8], F32)

    with ExitStack() as ctx:
        def sb(name, shape, dt):
            return ctx.enter_context(nc.sbuf_tensor(name, shape, dt))

        x_t = sb("x_t", [128, NCH * T], F32)
        qin = [sb(f"qin{i}", [128, QTILE], F32) for i in range(2)]
        kvp = [sb(f"kvp{i}", [128, T], F32) for i in range(2)]
        tmp2 = [sb(f"tmp2_{i}", [128, T], F32) for i in range(2)]
        rowb = sb("rowb", [1, 2048], F32)
        srow_g = sb("srow_g", [1, 8], F32)
        rrow = sb("rrow", [1, 8], F32)
        srow2 = sb("srow2", [1, 8], F32)
        effrow = sb("effrow", [1, 8], F32)
        ones_col = sb("ones_col", [128, 1], F32)
        ones_row = sb("ones_row", [1, 128], F32)
        carry = sb("carry", [128, NCH], F32)
        cc_sc = sb("cc_sc", [128, NCH], F32)
        invc = sb("invc", [128, 4], F32)
        sc3 = sb("sc3", [128, 4], F32)
        inv1 = [sb(f"inv1_{i}", [128, 1], F32) for i in range(3)]
        effc = sb("effc", [128, 4], F32)
        redcol = sb("redcol", [128, 48], F32)
        redm = sb("redm", [128, 4], F32)
        g8t = sb("g8t", [8, 8], F32)
        bqc = sb("bqc", [128, NCH], F32)
        bkc = sb("bkc", [128, NCH], F32)
        bvc = sb("bvc", [128, NCH], F32)
        boc = sb("boc", [128, NCH], F32)
        cstc = sb("cstc", [128, 8], F32)
        selt = sb("selt", [128, NCH], F32)
        allg = sb("allg", [128, 128], F32)

        bp = sb("bp", [128, PB_N], BF16)
        ones_sc = sb("ones_sc", [128, T], BF16)

        pb = [ctx.enter_context(nc.psum_tensor(f"pb{i}", [128, T], F32))
              for i in range(8)]

        kb = KB(nc, ctx)

        def bpv(off, n):
            return bp[:, off:off + n]

        def xn_v(c, tl):
            return bpv(XN0 + c * 2048 + tl * T, T)

        def x_c(c):
            return x_t[:, c * T:(c + 1) * T]

        t_ones_sc = kb.dve(lambda e: e.memset(ones_sc[:, :], 1.0))
        t_ones_col = kb.dve(lambda e: e.memset(ones_col[:, :], 1.0))
        t_ones_row = kb.dve(lambda e: e.memset(ones_row[:, :], 1.0))
        t_carry0 = kb.dve(lambda e: e.memset(carry[:, :], 0.0))
        t_srow0 = kb.dve(lambda e: e.memset(srow_g[:, :], 0.0))

        kb.dma("pool", lambda e: e.dma_start(out=cstc[:, :], in_=cst_d), "cld")
        kb.dma("pool", lambda e: e.dma_start(out=bqc[:, :], in_=bq_d), "cld")
        kb.dma("pool", lambda e: e.dma_start(out=bkc[:, :], in_=bk_d), "cld")
        kb.dma("pool", lambda e: e.dma_start(out=bvc[:, :], in_=bv_d), "cld")
        kb.dma("pool", lambda e: e.dma_start(out=boc[:, :], in_=bo_d), "cld")
        tok_cld = kb.dma("pool", lambda e: e.dma_start(out=selt[:, :],
                                                       in_=sel_d), "cld")

        qin_war = [None, None]
        t_red = []

        def scale_load(i):
            sl = i % 2
            return kb.dma("act", lambda e, i=i, sl=sl: e.dma_start(
                out=qin[sl][:, :], in_=wsl_d[i]), f"qld{sl}",
                waits=[qin_war[sl]])

        t_sl = [None] * 48
        t_sl[0] = scale_load(0)
        t_sl[1] = scale_load(1)
        for i in range(48):
            tr = kb.dve(lambda e, i=i, sl=i % 2: e.tensor_reduce(
                redcol[:, i:i + 1], qin[sl][:, :], AX.X, ALU.add,
                apply_absolute_value=True), waits=[t_sl[i]])
            qin_war[i % 2] = tr
            t_red.append(tr)
            if i + 2 < 48:
                t_sl[i + 2] = scale_load(i + 2)
        t_redm = []
        for m in range(3):
            t_redm.append(kb.dve(lambda e, m=m: e.tensor_reduce(
                redm[:, m:m + 1], redcol[:, 16 * m:16 * (m + 1)], AX.X,
                ALU.add), waits=t_red[16 * m:16 * (m + 1)]))
        t_smm = [None] * 3
        for m in range(3):
            t_smm[m] = kb.pe(lambda e, m=m: e.matmul(
                pb[m][0:1, 0:1], redm[:, m:m + 1], ones_col[:, 0:1],
                start=True, stop=True), waits=[t_redm[m], t_ones_col])
        t_srow = None
        for m in range(3):
            t_srow = kb.act(lambda e, m=m: e.copy(
                srow_g[0:1, m:m + 1], pb[m][0:1, 0:1]),
                waits=[t_smm[m], t_srow0])
        t_rin = kb.dma("syncq", lambda e: e.dma_start(out=rin.ap(),
                                                      in_=srow_g[:, :]),
                       "srin", waits=[t_srow])
        t_cc1 = kb.pool(lambda e: e.collective_compute(
            "AllGather", ALU.bypass, replica_groups=[list(range(8))],
            ins=[rin.ap().opt()], outs=[rgout.ap().opt()]), waits=[t_rin],
            sem="cc")
        t_g8 = kb.dma("pool", lambda e: e.dma_start(out=g8t[:, :],
                                                    in_=rgout.ap()),
                      "sg8", waits=[t_cc1])
        t_gsum = kb.pe(lambda e: e.matmul(
            pb[3][0:1, 0:8], ones_col[0:8, 0:1], g8t[0:8, :],
            start=True, stop=True), waits=[t_g8, t_ones_col])
        t_rr = kb.act(lambda e: e.copy(rrow[0:1, 0:8], pb[3][0:1, 0:8]),
                      waits=[t_gsum])
        t_sc2 = kb.dve(lambda e: e.tensor_scalar(
            srow2[:, :], rrow[:, :], 1.0 / NW, 1e-8, op0=ALU.mult,
            op1=ALU.add), waits=[t_rr])
        t_e0 = kb.dve(lambda e: e.tensor_scalar(
            effrow[0:1, 0:1], srow2[0:1, 0:1], 0.5, None, op0=ALU.mult),
            waits=[t_sc2])
        t_e1 = kb.dve(lambda e: e.tensor_tensor(
            effrow[0:1, 1:2], srow2[0:1, 1:2], srow2[0:1, 2:3], ALU.mult),
            waits=[t_sc2])
        t_e1 = kb.dve(lambda e: e.tensor_scalar(
            effrow[0:1, 1:2], effrow[0:1, 1:2], 0.25, None, op0=ALU.mult),
            waits=[t_e1])
        t_bsc = kb.pe(lambda e: e.matmul(
            pb[0][:, 0:3], ones_row[0:1, :], srow2[0:1, 0:3], start=True,
            stop=True), waits=[t_sc2, t_ones_row, t_srow])
        t_beff = kb.pe(lambda e: e.matmul(
            pb[1][:, 0:2], ones_row[0:1, :], effrow[0:1, 0:2], start=True,
            stop=True), waits=[t_e0, t_e1, t_ones_row])
        t_sc3 = kb.act(lambda e: e.copy(sc3[:, 0:3], pb[0][:, 0:3]),
                       waits=[t_bsc])
        t_invc = None
        for m in range(3):
            t_rm = kb.dve(lambda e, m=m: e.reciprocal(inv1[m][:, :],
                                                      sc3[:, m:m + 1]),
                          waits=[t_sc3])
            t_invc = kb.dve(lambda e, m=m: e.tensor_copy(invc[:, m:m + 1],
                                                         inv1[m][:, :]),
                            waits=[t_rm])
        t_effc = kb.act(lambda e: e.copy(effc[:, 0:2], pb[1][:, 0:2]),
                        waits=[t_beff])
        bank_free = {0: t_sc3, 1: t_effc, 2: t_srow, 3: t_rr,
                     4: None, 5: None, 6: None, 7: None}

        q_srcs = (wu_d, wg_d, wd_d)
        q_dsts = (wuq, wgq, wdq)
        qz_tok = {}
        q_jobs = []
        for e2 in range(EHC):
            q_jobs += [(0, 2 * e2), (0, 2 * e2 + 1),
                       (1, 2 * e2), (1, 2 * e2 + 1)]
        q_jobs += [(2, j) for j in range(2 * EHC)]
        q_state = {"i": 0, "qin_war": [t_red[46], t_red[47]],
                   "qb1_war": [None, None], "qb2_war": [None, None],
                   "li": 0, "loads": {}}

        def q_tile_src(m, t_i):
            d = q_srcs[m]
            return d[t_i // 2, (t_i % 2) * 64:(t_i % 2) * 64 + 64, :].rearrange(
                "r (x f) -> (r x) f", x=2)

        def q_tile_dst(m, t_i):
            d = q_dsts[m].ap()
            return d[t_i // 2, (t_i % 2) * 64:(t_i % 2) * 64 + 64, :].rearrange(
                "r (x f) -> (r x) f", x=2)

        def q_load(j):
            st_ = q_state
            m, t_i = q_jobs[j]
            sl = j % 2
            st_["loads"][j] = kb.dma(
                "act", lambda e, m=m, t_i=t_i, sl=sl: e.dma_start(
                    out=qin[sl][:, :], in_=q_tile_src(m, t_i)), f"qld{sl}",
                waits=[st_["qin_war"][sl]])

        def q_emit(n):
            st_ = q_state
            for _ in range(n):
                if st_["i"] >= len(q_jobs):
                    return
                j = st_["i"]
                while st_["li"] <= min(j + 1, len(q_jobs) - 1):
                    q_load(st_["li"])
                    st_["li"] += 1
                m, t_i = q_jobs[j]
                sl = j % 2
                qb1 = bpv(QBo + sl * QTILE, QTILE)
                qb2 = bpv(QB2o + sl * QTILE, QTILE)
                tl_ = st_["loads"].pop(j)
                tb1 = kb.act(lambda e, m=m, sl=sl, qb1=qb1: e.activation(
                    qb1, qin[sl][:, :], AF.Sign,
                    scale=invc[:, m:m + 1], bias=cstc[:, 2:3]),
                    waits=[tl_, t_invc, st_["qb1_war"][sl], tok_cld])
                tb2 = kb.act(lambda e, m=m, sl=sl, qb2=qb2: e.activation(
                    qb2, qin[sl][:, :], AF.Sign,
                    scale=invc[:, m:m + 1], bias=cstc[:, 3:4]),
                    waits=[tl_, t_invc, st_["qb2_war"][sl]])
                st_["qin_war"][sl] = tb2
                tq = kb.dve(lambda e, qb1=qb1, qb2=qb2: e.tensor_tensor(
                    qb1, qb1, qb2, ALU.add), waits=[tb1, tb2])
                st_["qb2_war"][sl] = tq
                tst = kb.dma("syncq", lambda e, m=m, t_i=t_i, qb1=qb1:
                             e.dma_start(out=q_tile_dst(m, t_i), in_=qb1),
                             f"qz{sl}", waits=[tq])
                st_["qb1_war"][sl] = tst
                qz_tok[(m, t_i)] = tst
                st_["i"] += 1

        def q_emit_until(n):
            while q_state["i"] < min(n, len(q_jobs)):
                q_emit(1)

        st = {
            "bank_free": bank_free,
            "x_free": None,
            "rowb_free": None,
            "tmp_free": None,
            "xn_tok": {},
            "xn_readers": [],
            "wA_war": [None, None],
            "wB_war": [None, None],
            "wC_war": [None, None],
            "er_war": [None] * 4,
            "sil_war": [None, None],
            "phik_war": [None] * HT,
            "pqo_war": [None] * 4,
            "kvo_war": [None] * 4,
            "hst_war": [None, None],
            "kvp_war": [None, None],
            "pkv_war": [None] * HT,
            "carry_last": t_carry0,
            "pq_tok": {},
            "kv_tok": {},
            "h_tok": {},
            "g_tok": {},
            "scan_last": None,
        }

        def x_load_tt(tt, waits):
            src = x_d[:, tt * T:(tt + 1) * T].rearrange("(c p) t -> p c t",
                                                        p=128)
            dst = x_t[:, :].rearrange("p (c t) -> p c t", t=T)
            return kb.dma("act", lambda e: e.dma_start(out=dst, in_=src),
                          "xld", waits=waits)

        def emit_ln(half, tl):
            xw = [st["x_free"]] if st["x_free"] else []
            tok_x = x_load_tt(half * HT + tl, xw)
            t_sum = None
            t_sq = [None] * NCH
            t_ssq = [None] * NCH
            for c in range(NCH):
                w0 = [tok_x, st["bank_free"][4]] if c == 0 else [tok_x]
                t_sum = kb.pe(lambda e, c=c: e.matmul(
                    pb[4][0:1, :], ones_col[:, 0:1], x_c(c),
                    start=(c == 0), stop=(c == NCH - 1)),
                    waits=w0 + [t_ones_col])
                t_sq[c] = kb.act(lambda e, c=c: e.square(
                    tmp2[c % 2][:, :], x_c(c)),
                    waits=[tok_x, t_ssq[c - 2] if c >= 2 else st["tmp_free"]])
                wb = [st["bank_free"][5]] if c == 0 else []
                t_ssq[c] = kb.pe(lambda e, c=c: e.matmul(
                    pb[5][0:1, :], ones_col[:, 0:1], tmp2[c % 2][:, :],
                    start=(c == 0), stop=(c == NCH - 1)),
                    waits=wb + [t_sq[c]])
            rw = [st["rowb_free"]] if st["rowb_free"] else []
            t_m = kb.act(lambda e: e.activation(
                rowb[0:1, 0:512], pb[4][0:1, :], AF.Copy,
                scale=cstc[0:1, 0:1]), waits=[t_sum, tok_cld] + rw)
            t_msq = kb.act(lambda e: e.activation(
                rowb[0:1, 512:1024], pb[5][0:1, :], AF.Copy,
                scale=cstc[0:1, 0:1]), waits=[t_ssq[NCH - 1]] + rw)
            t_m2 = kb.dve(lambda e: e.tensor_tensor(
                rowb[0:1, 1024:1536], rowb[0:1, 0:512], rowb[0:1, 0:512],
                ALU.mult), waits=[t_m] + rw)
            t_var = kb.dve(lambda e: e.tensor_tensor(
                rowb[0:1, 512:1024], rowb[0:1, 512:1024],
                rowb[0:1, 1024:1536], ALU.subtract), waits=[t_msq, t_m2])
            t_lnv = kb.act(lambda e: e.activation(
                rowb[0:1, 1536:2048], rowb[0:1, 512:1024], AF.Ln,
                bias=cstc[0:1, 1:2]), waits=[t_var])
            t_r = kb.act(lambda e: e.activation(
                rowb[0:1, 1536:2048], rowb[0:1, 1536:2048], AF.Exp,
                scale=cstc[0:1, 2:3]), waits=[t_lnv])
            t_mr = kb.dve(lambda e: e.tensor_tensor(
                rowb[0:1, 1024:1536], rowb[0:1, 0:512], rowb[0:1, 1536:2048],
                ALU.mult), waits=[t_r, t_m, t_m2])
            t_Rb = kb.pe(lambda e: e.matmul(
                pb[6][:, :], ones_row[0:1, :], rowb[0:1, 1536:2048],
                start=True, stop=True),
                waits=[t_r, t_ones_row, st["bank_free"][6]])
            t_Mb = kb.pe(lambda e: e.matmul(
                pb[7][:, :], ones_row[0:1, :], rowb[0:1, 1024:1536],
                start=True, stop=True), waits=[t_mr, st["bank_free"][7]])
            st["rowb_free"] = t_Mb
            xn_war = list(st["xn_readers"])
            t_t1 = [None] * NCH
            t_xn = [None] * NCH
            for c in range(NCH):
                t_t1[c] = kb.dve(lambda e, c=c: e.tensor_tensor(
                    tmp2[c % 2][:, :], x_c(c), pb[6][:, :], ALU.mult),
                    waits=[t_Rb, tok_x, t_ssq[NCH - 1],
                           t_xn[c - 2] if c >= 2 else None])
                t_xn[c] = kb.dve(lambda e, c=c, tl=tl: e.tensor_tensor(
                    xn_v(c, tl), tmp2[c % 2][:, :], pb[7][:, :],
                    ALU.subtract),
                    waits=[t_Mb, t_t1[c]] + (xn_war if c < 2 else []))
                st["xn_tok"][(half, c, tl)] = t_xn[c]
            st["tmp_free"] = t_xn[NCH - 1]
            st["x_free"] = t_t1[NCH - 1]
            st["bank_free"][4] = t_m
            st["bank_free"][5] = t_msq
            st["bank_free"][6] = t_t1[NCH - 1]
            st["bank_free"][7] = t_xn[NCH - 1]

        def mm_group(bank, wview, rhs_of_c, extra_waits, nk=NCH):
            tokl = None
            for c in range(nk):
                w = list(extra_waits) if c == 0 else []
                tokl = kb.pe(lambda e, c=c: e.matmul(
                    pb[bank][:, :], wview[:, c * 128:(c + 1) * 128],
                    rhs_of_c(c), start=(c == 0), stop=(c == nk - 1)),
                    waits=w, inc=(c == nk - 1))
            return tokl

        def emit_qkv(half, qpace, qpace_kv=None):
            if qpace_kv is None:
                qpace_kv = qpace
            xn_tok = st["xn_tok"]

            def wA_load(wdram, oc, sl):
                return kb.dma("pool", lambda e, oc=oc, sl=sl, wdram=wdram:
                              e.dma_start(out=bpv(WAo + sl * D, D),
                                          in_=wdram[oc]),
                              f"wA{sl}", waits=[st["wA_war"][sl]])

            def phi_epilogue(oc, tl, bank, bias, dst, dst_war_key, war_list):
                t_e = kb.act(lambda e, oc=oc, tl=tl, bank=bank, bias=bias:
                             e.activation(
                                 bpv(ERo + tl * T, T), pb[bank][:, :],
                                 AF.Exp, bias=bias[:, oc:oc + 1]),
                             waits=[war_list[0], st["er_war"][tl], tok_cld])
                t_r2 = kb.act(lambda e, oc=oc, tl=tl, bank=bank, bias=bias:
                              e.activation(
                                  bpv(SILo + (tl % 2) * T, T),
                                  pb[bank][:, :], AF.Relu,
                                  bias=bias[:, oc:oc + 1]),
                              waits=[war_list[0], st["sil_war"][tl % 2]])
                st["bank_free"][bank] = t_r2
                t_phi = kb.dve(lambda e, tl=tl, dst=dst:
                               e.scalar_tensor_tensor(
                                   dst, bpv(ERo + tl * T, T), 1.0,
                                   bpv(SILo + (tl % 2) * T, T),
                                   op0=ALU.min, op1=ALU.add),
                               waits=[t_e, t_r2, dst_war_key])
                st["er_war"][tl] = t_phi
                st["sil_war"][tl % 2] = t_phi
                return t_phi

            t_w = {0: wA_load(wq_d, 0, 0), 1: wA_load(wq_d, 1, 1)}
            for oc in range(NCH):
                sl = oc % 2
                grp_last = None
                for tl in range(HT):
                    bank = tl
                    ew = [t_w[oc], xn_tok[(half, 15, tl)],
                          st["bank_free"][bank]]
                    grp = mm_group(bank, bpv(WAo + sl * D, D),
                                   lambda c, tl=tl: xn_v(c, tl), ew)
                    st["bank_free"][bank] = grp
                    t_phi = phi_epilogue(oc, tl, bank, bqc,
                                         bpv(PQOo + tl * T, T),
                                         st["pqo_war"][tl], [grp])
                    t_pst = kb.dma("syncq", lambda e, oc=oc, half=half, tl=tl:
                                   e.dma_start(
                                       out=phiq_s.ap()[oc][
                                           :, (half * HT + tl) * T:
                                           (half * HT + tl + 1) * T],
                                       in_=bpv(PQOo + tl * T, T)),
                                   f"spq{tl}", waits=[t_phi])
                    st["pqo_war"][tl] = t_pst
                    st["pq_tok"][oc] = t_pst
                    grp_last = grp
                st["wA_war"][sl] = grp_last
                if oc + 2 < NCH:
                    t_w[oc + 2] = wA_load(wq_d, oc + 2, sl)
                if oc >= 2:
                    q_emit(qpace)

            t_wk = {0: wA_load(wk_d, 0, 0)}
            t_wv = {0: wA_load(wv_d, 0, 1)}
            for oc in range(NCH):
                klast = None
                for tl in range(HT):
                    bank = tl
                    ew = [t_wk[oc], xn_tok[(half, 15, tl)],
                          st["bank_free"][bank]]
                    grp = mm_group(bank, bpv(WAo + 0 * D, D),
                                   lambda c, tl=tl: xn_v(c, tl), ew)
                    st["bank_free"][bank] = grp
                    phi_epilogue(oc, tl, bank, bkc, bpv(PHIKo + tl * T, T),
                                 st["phik_war"][tl], [grp])
                    st["phik_war"][tl] = st["er_war"][tl]
                    klast = grp
                st["wA_war"][0] = klast
                if oc + 1 < NCH:
                    t_wk[oc + 1] = wA_load(wk_d, oc + 1, 0)
                vlast = None
                for tl in range(HT):
                    bank = tl
                    ew = [t_wv[oc], st["bank_free"][bank]]
                    grp = mm_group(bank, bpv(WAo + 1 * D, D),
                                   lambda c, tl=tl: xn_v(c, tl), ew)
                    t_pkv = kb.dve(lambda e, oc=oc, tl=tl, bank=bank:
                                   e.scalar_tensor_tensor(
                                       bpv(PKVo + tl * T, T),
                                       pb[bank][:, :], bvc[:, oc:oc + 1],
                                       bpv(PHIKo + tl * T, T),
                                       op0=ALU.add, op1=ALU.mult),
                                   waits=[grp, st["phik_war"][tl],
                                          st["pkv_war"][tl], tok_cld])
                    st["bank_free"][bank] = t_pkv
                    st["phik_war"][tl] = t_pkv
                    vlast = grp
                    ssl = tl % 2
                    t_scan = kb.dve(lambda e, oc=oc, tl=tl, ssl=ssl:
                                    e.tensor_tensor_scan(
                                        kvp[ssl][:, :], ones_sc[:, :],
                                        bpv(PKVo + tl * T, T),
                                        carry[:, oc:oc + 1],
                                        op0=ALU.mult, op1=ALU.add),
                                    waits=[t_pkv, t_ones_sc,
                                           st["carry_last"],
                                           st["kvp_war"][ssl]], force=True)
                    t_carry = kb.dve(lambda e, oc=oc, ssl=ssl: e.tensor_copy(
                        carry[:, oc:oc + 1], kvp[ssl][:, T - 1:T]),
                        waits=[t_scan], force=True)
                    st["carry_last"] = t_carry
                    st["pkv_war"][tl] = t_scan
                    t_kvo = kb.dve(lambda e, ssl=ssl, tl=tl: e.tensor_scalar(
                        bpv(KVOo + tl * T, T), kvp[ssl][:, :], SCALE, None,
                        op0=ALU.mult), waits=[t_scan, st["kvo_war"][tl]])
                    st["kvp_war"][ssl] = t_kvo
                    t_kst = kb.dma("syncq", lambda e, oc=oc, half=half, tl=tl:
                                   e.dma_start(
                                       out=kv_s.ap()[oc][
                                           :, (half * HT + tl) * T:
                                           (half * HT + tl + 1) * T],
                                       in_=bpv(KVOo + tl * T, T)),
                                   f"skv{tl}", waits=[t_kvo])
                    st["kvo_war"][tl] = t_kst
                    st["kv_tok"][oc] = t_kst
                st["wA_war"][1] = vlast
                if oc + 1 < NCH:
                    t_wv[oc + 1] = wA_load(wv_d, oc + 1, 1)
                st["xn_readers"] = [vlast]
                q_emit(qpace_kv)
            st["scan_last"] = st["carry_last"]

        def emit_ug(half, qpace, wqueue="pool"):
            xn_tok = st["xn_tok"]

            def wB_load(eh):
                sl = eh % 2
                return kb.dma(wqueue, lambda e, eh=eh, sl=sl: e.dma_start(
                    out=bpv(WBo + sl * D, D), in_=wuq.ap()[eh]), f"wB{sl}",
                    waits=[st["wB_war"][sl], qz_tok.get((0, 2 * eh)),
                           qz_tok.get((0, 2 * eh + 1))])

            def wC_load(eh):
                sl = eh % 2
                return kb.dma(wqueue, lambda e, eh=eh, sl=sl: e.dma_start(
                    out=bpv(WCo + sl * D, D), in_=wgq.ap()[eh]), f"wC{sl}",
                    waits=[st["wC_war"][sl], qz_tok.get((1, 2 * eh)),
                           qz_tok.get((1, 2 * eh + 1))])

            t_wu = {0: wB_load(0), 1: wB_load(1)}
            t_wg = {0: wC_load(0), 1: wC_load(1)}
            for eh in range(EHC):
                sl = eh % 2
                hsl = eh % 2
                up_last = gate_last = None
                t_h = None
                for tl in range(HT):
                    ub = tl % 2
                    gb = 2 + tl % 2
                    ew = [t_wu[eh], xn_tok[(half, 15, tl)],
                          st["bank_free"][ub]]
                    up = mm_group(ub, bpv(WBo + sl * D, D),
                                  lambda c, tl=tl: xn_v(c, tl), ew)
                    t_sil = kb.act(lambda e, ub=ub, tl=tl: e.activation(
                        bpv(SILo + (tl % 2) * T, T), pb[ub][:, :], AF.Silu,
                        scale=effc[:, 0:1]),
                        waits=[up, t_effc, st["sil_war"][tl % 2]])
                    st["bank_free"][ub] = t_sil
                    ewg = [t_wg[eh], st["bank_free"][gb]]
                    gate = mm_group(gb, bpv(WCo + sl * D, D),
                                    lambda c, tl=tl: xn_v(c, tl), ewg)
                    t_h = kb.dve(lambda e, gb=gb, tl=tl, hsl=hsl:
                                 e.tensor_tensor(
                                     bpv(HSTo + hsl * 2048 + tl * T, T),
                                     bpv(SILo + (tl % 2) * T, T),
                                     pb[gb][:, :], ALU.mult),
                                 waits=[t_sil, gate, st["hst_war"][hsl]])
                    st["sil_war"][tl % 2] = t_h
                    st["bank_free"][gb] = t_h
                    up_last, gate_last = up, gate
                t_hst = kb.dma("syncq", lambda e, eh=eh, half=half, hsl=hsl:
                               e.dma_start(
                                   out=h_s.ap()[eh][:, half * HT * T:
                                                    (half + 1) * HT * T],
                                   in_=bpv(HSTo + hsl * 2048, 2048)),
                               f"sh{hsl}", waits=[t_h])
                st["hst_war"][hsl] = t_hst
                st["h_tok"][(eh, half)] = t_hst
                st["wB_war"][sl] = up_last
                st["wC_war"][sl] = gate_last
                if eh + 2 < EHC:
                    t_wu[eh + 2] = wB_load(eh + 2)
                    t_wg[eh + 2] = wC_load(eh + 2)
                st["xn_readers"] = [gate_last]
                q_emit(qpace)
                yield eh

        for tl in range(HT):
            emit_ln(0, tl)
        emit_qkv(0, qpace=3, qpace_kv=4)
        for _ in emit_ug(0, qpace=3):
            pass

        for tl in range(HT):
            emit_ln(1, tl)
        emit_qkv(1, qpace=0)

        t_tin = None
        for c in range(NCH):
            t_tin = kb.dma("syncq", lambda e, c=c: e.dma_start(
                out=tin.ap()[c:c + 1, :].rearrange("a b -> b a"),
                in_=carry[:, c:c + 1]), "stin", waits=[st["scan_last"]])
        t_cc2 = kb.pool(lambda e: e.collective_compute(
            "AllGather", ALU.bypass, replica_groups=[list(range(8))],
            ins=[tin.ap().opt()], outs=[tout.ap().opt()]),
            waits=[t_tin, t_cc1], sem="cc")
        t_allg = kb.dma("pool", lambda e: e.dma_start(
            out=allg[:, :], in_=tout.ap()), "sg8", waits=[t_cc2])

        t_ccsc = None
        for eh in emit_ug(1, qpace=2, wqueue="act"):
            if eh == 32:
                t_selmm = kb.pe(lambda e: e.matmul(
                    pb[5][:, 0:NCH], allg[:, :], selt[:, :], start=True,
                    stop=True), waits=[t_allg, tok_cld, st["bank_free"][5]])
                t_ccsc = kb.act(lambda e: e.copy(cc_sc[:, :],
                                                 pb[5][:, 0:NCH]),
                                waits=[t_selmm])
                st["bank_free"][5] = t_ccsc

        q_emit_until(len(q_jobs))

        region_free = ([st["wB_war"][0], st["wB_war"][1], st["wC_war"][0],
                        st["wC_war"][1], st["hst_war"][0], st["hst_war"][1],
                        q_state["qb1_war"][0], q_state["qb1_war"][1],
                        q_state["qb2_war"][0], q_state["qb2_war"][1],
                        st["sil_war"][0], st["sil_war"][1]]
                       + list(st["pkv_war"])
                       + list(st["er_war"]) + list(st["phik_war"])
                       + list(st["pqo_war"]) + list(st["kvo_war"]))
        region_free = [t for t in region_free if t is not None]
        xn_free = list(st["xn_readers"])

        def wd_tile(ocl, j):
            return bpv(WDo + (ocl * 8 + j) * QTILE, QTILE)

        dn = {
            "htt_war": [None] * HSLOT,
            "wd_war": {},
            "gdo_war": [None] * 4,
            "ctx_tok": {},
            "ctx_kv_war": [None, None],
            "cstg_war": [None, None],
            "fill_tok": {},
        }

        def emit_ctx(c, extra_waits=()):
            co = ctx_off(c)
            t_kv = kb.dma("act", lambda e, c=c, co=co: e.dma_start(
                out=bpv(co, 4096), in_=kv_s.ap()[c]), f"ckv{c % 2}",
                waits=[st["kv_tok"][c], dn["ctx_kv_war"][c % 2]]
                + list(extra_waits))
            t_ctx = None
            for hh in range(4):
                qsl = hh % 2
                t_pq = kb.dma("syncq", lambda e, c=c, hh=hh, qsl=qsl:
                              e.dma_start(
                                  out=bpv(CSTGo + qsl * 1024, 1024),
                                  in_=phiq_s.ap()[c][:, hh * 1024:
                                                     (hh + 1) * 1024]),
                              f"cpq{qsl}", waits=[st["pq_tok"][c],
                                                  dn["cstg_war"][qsl]])
                t_ctx = kb.dve(lambda e, c=c, co=co, hh=hh, qsl=qsl:
                               e.scalar_tensor_tensor(
                                   bpv(co + hh * 1024, 1024),
                                   bpv(co + hh * 1024, 1024),
                                   cc_sc[:, c:c + 1],
                                   bpv(CSTGo + qsl * 1024, 1024),
                                   op0=ALU.add, op1=ALU.mult),
                               waits=[t_kv, t_pq, t_ccsc])
                dn["cstg_war"][qsl] = t_ctx
            dn["ctx_kv_war"][c % 2] = t_ctx
            dn["ctx_tok"][c] = t_ctx

        ctx_queue = list(range(8, NCH))
        h_queues = ["act", "pool"]

        def h_load(ec, td, war):
            slot = ec % HSLOT
            return kb.dma(
                h_queues[ec % 2],
                lambda e, ec=ec, td=td, slot=slot: e.dma_start(
                    out=bpv(HTTo + slot * TD, TD),
                    in_=h_s.ap()[ec][:, td * TD:(td + 1) * TD]),
                f"ht{slot}", waits=[st["h_tok"][(ec, td // 8)], war])

        for oq in range(4):
            fill_prev = {}
            nf = 0
            for j in range(8):
                for ocl in range(4):
                    oc = oq * 4 + ocl
                    k = oc * 8 + j
                    fsl = nf % 8
                    w = [qz_tok[(2, k)], fill_prev.get(fsl),
                         dn["wd_war"].get((ocl, j))]
                    if oq == 0 and nf == 0:
                        w += region_free + xn_free
                    tf = kb.dma("syncq",
                                lambda e, ocl=ocl, j=j, k=k: e.dma_start(
                                    out=wd_tile(ocl, j),
                                    in_=q_tile_dst(2, k)),
                                f"wdf{fsl}", waits=w)
                    dn["fill_tok"][(oq, ocl, j)] = tf
                    fill_prev[fsl] = tf
                    nf += 1
            for td in range(NTD):
                t_hld = {}
                for ec in range(HSLOT):
                    t_hld[ec] = h_load(ec, td, dn["htt_war"][ec % HSLOT])
                for ec in range(EHC):
                    slot = ec % HSLOT
                    j, e8 = ec // 8, ec % 8
                    last8 = None
                    for ocl in range(4):
                        bank = (td % 2) * 4 + ocl
                        w = [t_hld[ec]]
                        if ec == 0:
                            w += [dn["fill_tok"][(oq, ocl, 0)],
                                  st["bank_free"][bank]]
                        elif e8 == 0:
                            w.append(dn["fill_tok"][(oq, ocl, j)])
                        last8 = kb.pe(
                            lambda e, ocl=ocl, j=j, e8=e8, slot=slot,
                            bank=bank, ec=ec: e.matmul(
                                pb[bank][:, 0:TD],
                                wd_tile(ocl, j)[:, e8 * 128:(e8 + 1) * 128],
                                bpv(HTTo + slot * TD, TD),
                                start=(ec == 0), stop=(ec == EHC - 1)),
                            waits=w, inc=(ocl == 3))
                    for ocl in range(4):
                        dn["wd_war"][(ocl, j)] = last8
                    dn["htt_war"][slot] = last8
                    if ec + HSLOT < EHC:
                        t_hld[ec + HSLOT] = h_load(ec + HSLOT, td, last8)
                for ocl in range(4):
                    oc = oq * 4 + ocl
                    bank = (td % 2) * 4 + ocl
                    t_g = kb.act(lambda e, bank=bank, ocl=ocl: e.activation(
                        bpv(GDOo + ocl * TD, TD), pb[bank][:, 0:TD],
                        AF.Silu, scale=effc[:, 1:2]),
                        waits=[dn["wd_war"][(ocl, 7)], t_effc,
                               dn["gdo_war"][ocl]])
                    st["bank_free"][bank] = t_g
                    t_gst = kb.dma("syncq", lambda e, oc=oc, td=td, ocl=ocl:
                                   e.dma_start(
                                       out=g_s.ap()[oc][:, td * TD:
                                                        (td + 1) * TD],
                                       in_=bpv(GDOo + ocl * TD, TD)),
                                   f"sgd{ocl}", waits=[t_g])
                    dn["gdo_war"][ocl] = t_gst
                    st["g_tok"][oc] = t_gst
                if ctx_queue and td % 3 == 2:
                    c = ctx_queue.pop(0)
                    emit_ctx(c, extra_waits=(xn_free if c == 8 else ()))
        while ctx_queue:
            c = ctx_queue.pop(0)
            emit_ctx(c, extra_waits=(xn_free if c == 8 else ()))

        down_done = [dn["wd_war"][(ocl, 7)] for ocl in range(4)]
        htt_done = [t for t in dn["htt_war"] if t is not None]
        for c in range(8):
            emit_ctx(c, extra_waits=(down_done if c < 2 else ()))

        fin = {
            "wo_war": None,
            "x_war": [None, None],
            "gin_war": None,
            "sgt_war": [None, None],
            "y_tok": [None, None],
        }
        for oc in range(NCH):
            sl = oc % 2
            t_wo = kb.dma("pool", lambda e, oc=oc: e.dma_start(
                out=bpv(WOo, D), in_=wo_d[oc]), "wo0",
                waits=[fin["wo_war"]] + (htt_done if oc == 0 else []))
            t_x = kb.dma("act", lambda e, oc=oc, sl=sl: e.dma_start(
                out=x_t[:, sl * 4096:(sl + 1) * 4096],
                in_=x_d[oc * 128:(oc + 1) * 128, :]), f"fx{sl}",
                waits=[fin["y_tok"][sl], fin["x_war"][sl]]
                + ([st["x_free"], st["tmp_free"]] if oc < 2 else []))
            t_gin = kb.dma("pool", lambda e, oc=oc: e.dma_start(
                out=bpv(GINo, 4096), in_=g_s.ap()[oc]), "fg",
                waits=[st["g_tok"][oc], fin["gin_war"]]
                + (htt_done if oc == 0 else []))
            t_sgt = None
            t_out = [None] * NTT
            c_order = list(range(8, NCH)) + list(range(8))
            for tt in range(NTT):
                bank = 4 + tt % 4
                grp = None
                for i, c in enumerate(c_order):
                    w = [t_wo, st["bank_free"][bank]] if i == 0 else []
                    if oc == 0:
                        w.append(dn["ctx_tok"][c])
                    grp = kb.pe(lambda e, c=c, i=i, bank=bank, tt=tt:
                                e.matmul(
                                    pb[bank][:, :],
                                    bpv(WOo + c * 128, 128),
                                    bpv(ctx_off(c) + tt * T, T),
                                    start=(i == 0), stop=(i == NCH - 1)),
                                waits=w, inc=(i == NCH - 1))
                ssl = tt % 2
                sgw = [grp, t_gin, fin["sgt_war"][ssl]]
                if oc == 0 and tt == 0:
                    sgw += [t for t in dn["gdo_war"] if t is not None]
                t_sgt = kb.dve(lambda e, oc=oc, tt=tt, ssl=ssl, bank=bank:
                               e.scalar_tensor_tensor(
                                   bpv(SGTo + ssl * T, T), pb[bank][:, :],
                                   boc[:, oc:oc + 1],
                                   bpv(GINo + tt * T, T),
                                   op0=ALU.add, op1=ALU.mult),
                               waits=sgw)
                st["bank_free"][bank] = t_sgt
                t_out[tt] = kb.dve(lambda e, tt=tt, sl=sl, ssl=ssl:
                                   e.tensor_tensor(
                                       x_t[:, sl * 4096 + tt * T:
                                           sl * 4096 + (tt + 1) * T],
                                       x_t[:, sl * 4096 + tt * T:
                                           sl * 4096 + (tt + 1) * T],
                                       bpv(SGTo + ssl * T, T), ALU.add),
                                   waits=[t_sgt, t_x])
                fin["sgt_war"][ssl] = t_out[tt]
            fin["wo_war"] = grp
            fin["gin_war"] = t_sgt
            fin["x_war"][sl] = t_out[NTT - 1]
            t_y = kb.dma("syncq", lambda e, oc=oc, sl=sl: e.dma_start(
                out=y_d[oc], in_=x_t[:, sl * 4096:(sl + 1) * 4096]),
                f"syo{sl}", waits=[t_out[NTT - 1]])
            fin["y_tok"][sl] = t_y

        kb.cnt["cc"] += 1

        def _final_sync(e):
            e.wait_ge(kb.sems["syo0"], kb.cnt["syo0"])
            e.wait_ge(kb.sems["syo1"], kb.cnt["syo1"])
            e.nop().then_inc(kb.sems["cc"], 1)
        kb.op("syncq", _final_sync)

        def _cleanup(e):
            for sname in kb.sems:
                e.wait_ge(kb.sems[sname], kb.cnt[sname])
            if _CLEANUP != "waits_only":
                for sname in kb.sems:
                    e.sem_clear(kb.sems[sname])
        if _CLEANUP:
            kb.op("pool", _cleanup)

        if _ENTRY_CLEAR:
            nums = sorted(h.num for h in kb.sems.values())
            nc.all_engine_barrier()
            nc.gpsimd.dma_reset(range(nums[0], nums[-1] + 1))
            nc.gpsimd.sem_clear(range(nums[0], nums[-1] + 1))
            nc.all_engine_barrier()
        with nc.Block() as block:
            @block.tensor
            def _(eng):
                for f in kb.plan["pe"]:
                    f(eng)

            @block.scalar
            def _(eng):
                for f in kb.plan["act"]:
                    f(eng)

            @block.vector
            def _(eng):
                for f in kb.plan["dve"]:
                    f(eng)

            @block.gpsimd
            def _(eng):
                for f in kb.plan["pool"]:
                    f(eng)

            @block.sync
            def _(eng):
                for f in kb.plan["syncq"]:
                    f(eng)

    return nc


def _prep_shared(Wq, Wk, Wv, Wo, Wu, Wg, Wd, bq, bk, bv, bo):
    def attn_stream(W):
        WT = np.ascontiguousarray(np.asarray(W, np.float32).T)
        return np.ascontiguousarray(
            WT.reshape(16, 128, 16, 128).transpose(2, 1, 0, 3)
        ).reshape(16, 128, 2048)

    def upgate_stream(W):
        WT = np.ascontiguousarray(np.asarray(W, np.float32).T)
        return np.ascontiguousarray(
            WT.reshape(16, 128, 64, 128).transpose(2, 1, 0, 3)
        ).reshape(64, 128, 2048)

    def down_stream(W):
        WT = np.ascontiguousarray(np.asarray(W, np.float32).T)
        A = WT.reshape(64, 128, 16, 128).transpose(2, 0, 1, 3)
        A = A.reshape(16, 8, 8, 128, 128).transpose(0, 1, 3, 2, 4)
        return np.ascontiguousarray(A).reshape(64, 128, 2048)

    def bcol(b):
        return np.ascontiguousarray(np.asarray(b, np.float32).reshape(16, 128).T)

    d = {
        "wq": attn_stream(Wq), "wk": attn_stream(Wk), "wv": attn_stream(Wv),
        "wo": attn_stream(Wo),
        "wu": upgate_stream(Wu), "wg": upgate_stream(Wg), "wd": down_stream(Wd),
        "bq": bcol(bq), "bk": bcol(bk), "bv": bcol(bv), "bo": bcol(bo),
        "cst": np.tile(np.array([[1.0 / 2048, 1e-5, -0.5, 0.5, 0, 0, 0, 0]],
                                np.float32), (128, 1)),
    }
    return d


def kernel(L0_attn_x, L0_attn_Wq, L0_attn_bq, L0_attn_Wk, L0_attn_bk,
           L0_attn_Wv, L0_attn_bv, L0_attn_Wo, L0_attn_bo,
           L0_ffn_W_up, L0_ffn_W_gate, L0_ffn_W_down):
    x = np.asarray(L0_attn_x, np.float32)
    shared = _prep_shared(L0_attn_Wq, L0_attn_Wk, L0_attn_Wv, L0_attn_Wo,
                          L0_ffn_W_up, L0_ffn_W_gate, L0_ffn_W_down,
                          L0_attn_bq, L0_attn_bk, L0_attn_bv, L0_attn_bo)

    wu_flat = shared["wu"].reshape(8192, 2048)
    wg_flat = shared["wg"].reshape(8192, 2048)
    wd_flat = shared["wd"].reshape(8192, 2048)

    in_maps = []
    for c in range(8):
        b, h = c // 2, c % 2
        x_fm = np.ascontiguousarray(x[b, h * 4096:(h + 1) * 4096, :].T)
        sel = np.zeros((128, 16), np.float32)
        if h == 1:
            r = c - 1
            for ch in range(16):
                sel[16 * r + ch, ch] = SCALE
        wsl = np.ascontiguousarray(np.stack([
            wu_flat[c * 1024:(c + 1) * 1024],
            wg_flat[c * 1024:(c + 1) * 1024],
            wd_flat[c * 1024:(c + 1) * 1024]])).reshape(48, 128, 1024)
        m = dict(shared)
        m["x"] = x_fm
        m["sel"] = sel
        m["wsl"] = wsl
        in_maps.append(m)

    nc = build_nc()
    res = run_bass_kernel_spmd(nc, in_maps, core_ids=list(range(8)))

    out = np.empty((4, 8192, 2048), np.float32)
    for c in range(8):
        b, h = c // 2, c % 2
        y = np.asarray(res.results[c]["y"])
        out_fm = y.reshape(2048, 4096)
        out[b, h * 4096:(h + 1) * 4096, :] = out_fm.T
    return out


# version 12 (final)
# speedup vs baseline: 1.0423x; 1.0071x over previous
import numpy as np
import concourse.bass as bass
import concourse.mybir as mybir
from concourse.bass_utils import run_bass_kernel_spmd
from contextlib import ExitStack

F32 = mybir.dt.float32
BF16 = mybir.dt.bfloat16
AF = mybir.ActivationFunctionType
ALU = mybir.AluOpType
AX = mybir.AxisListType

D = 2048
NCH = 16
T = 512
NTT = 8
HT = 4
EHC = 64
DE = 8192
NW = float(DE * D)
SCALE = 128.0 ** -0.5
QTILE = 1024
TD = 256
NTD = 16
HSLOT = 24

_CLEANUP = "waits_only"
_ENTRY_CLEAR = True

PB_N = 74752
XN0 = 0
WAo = 32768
WBo = 36864
WCo = 40960
QBo = 45056
ERo = 47104
SILo = 49152
PHIKo = 50176
PKVo = 52224
PQOo = 54272
KVOo = 56320
HSTo = 58368
QB2o = 62464
WDo = 32768
HTTo = 65536
GDOo = 71680
CSTGo = 72704
GINo = 65536
WOo = 69632
SGTo = 71680


def ctx_off(c):
    return ((c + 8) % 16) * 4096


class KB:
    ENGS = ("pe", "act", "dve", "pool", "syncq")
    OWN = {"pe": "pe", "act": "act", "dve": "dve", "pool": "pool_c"}

    def __init__(self, nc, ctx):
        self.nc = nc
        self.ctx = ctx
        self.plan = {e: [] for e in self.ENGS}
        self.sems = {}
        self.cnt = {}
        self.wm = {e: {} for e in self.ENGS}

    def sem(self, name):
        if name not in self.sems:
            self.sems[name] = self.ctx.enter_context(self.nc.semaphore(name))
            self.cnt[name] = 0
        return self.sems[name]

    def wait(self, eng, tok):
        if tok is None:
            return
        s, v, force = tok
        if self.OWN.get(eng) == s and not force:
            return
        if self.wm[eng].get(s, 0) >= v:
            return
        self.wm[eng][s] = v
        h = self.sems[s]
        self.plan[eng].append(lambda e, h=h, v=v: e.wait_ge(h, v))

    def op(self, eng, fn, sem=None, n=1, waits=(), force=False):
        for t in waits:
            self.wait(eng, t)
        if sem is None:
            self.plan[eng].append(fn)
            return None
        self.sem(sem)
        self.cnt[sem] += n
        v = self.cnt[sem]
        h = self.sems[sem]
        self.plan[eng].append(lambda e, fn=fn, h=h, n=n: fn(e).then_inc(h, n))
        return (sem, v, force)

    def pe(self, fn, waits=(), force=False, inc=True):
        if inc:
            return self.op("pe", fn, "pe", 1, waits, force)
        self.op("pe", fn, None, 1, waits, force)
        return None

    def act(self, fn, waits=(), force=True):
        return self.op("act", fn, "act", 1, waits, force)

    def dve(self, fn, waits=(), force=True):
        return self.op("dve", fn, "dve", 1, waits, force)

    def pool(self, fn, waits=(), sem="pool_c", n=1, force=True):
        return self.op("pool", fn, sem, n, waits, force)

    def dma(self, queue, fn, sem, waits=()):
        return self.op(queue, fn, sem, 16, waits, False)


def build_nc():
    nc = bass.Bass("TRN2", target_bir_lowering=False)

    x_d = nc.dram_tensor("x", [D, NTT * T], F32, kind="ExternalInput").ap()
    wq_d = nc.dram_tensor("wq", [NCH, 128, D], F32, kind="ExternalInput").ap()
    wk_d = nc.dram_tensor("wk", [NCH, 128, D], F32, kind="ExternalInput").ap()
    wv_d = nc.dram_tensor("wv", [NCH, 128, D], F32, kind="ExternalInput").ap()
    wo_d = nc.dram_tensor("wo", [NCH, 128, D], F32, kind="ExternalInput").ap()
    wu_d = nc.dram_tensor("wu", [EHC, 128, D], F32, kind="ExternalInput").ap()
    wg_d = nc.dram_tensor("wg", [EHC, 128, D], F32, kind="ExternalInput").ap()
    wd_d = nc.dram_tensor("wd", [EHC, 128, D], F32, kind="ExternalInput").ap()
    wsl_d = nc.dram_tensor("wsl", [48, 128, QTILE], F32, kind="ExternalInput").ap()
    bq_d = nc.dram_tensor("bq", [128, NCH], F32, kind="ExternalInput").ap()
    bk_d = nc.dram_tensor("bk", [128, NCH], F32, kind="ExternalInput").ap()
    bv_d = nc.dram_tensor("bv", [128, NCH], F32, kind="ExternalInput").ap()
    bo_d = nc.dram_tensor("bo", [128, NCH], F32, kind="ExternalInput").ap()
    cst_d = nc.dram_tensor("cst", [128, 8], F32, kind="ExternalInput").ap()
    sel_d = nc.dram_tensor("sel", [128, NCH], F32, kind="ExternalInput").ap()

    y_d = nc.dram_tensor("y", [NCH, 128, NTT * T], F32, kind="ExternalOutput").ap()

    wuq = nc.dram_tensor("wuq", [EHC, 128, D], BF16)
    wgq = nc.dram_tensor("wgq", [EHC, 128, D], BF16)
    wdq = nc.dram_tensor("wdq", [EHC, 128, D], BF16)
    phiq_s = nc.dram_tensor("phiq_s", [NCH, 128, NTT * T], BF16)
    kv_s = nc.dram_tensor("kv_s", [NCH, 128, NTT * T], BF16)
    h_s = nc.dram_tensor("h_s", [EHC, 128, NTT * T], BF16)
    g_s = nc.dram_tensor("g_s", [NCH, 128, NTT * T], BF16)
    tin = nc.dram_tensor("tin", [NCH, 128], F32)
    tout = nc.dram_tensor("tout", [8 * NCH, 128], F32)
    rin = nc.dram_tensor("rin", [1, 8], F32)
    rgout = nc.dram_tensor("rgout", [8, 8], F32)

    with ExitStack() as ctx:
        def sb(name, shape, dt):
            return ctx.enter_context(nc.sbuf_tensor(name, shape, dt))

        x_t = sb("x_t", [128, NCH * T], F32)
        qin = [sb(f"qin{i}", [128, QTILE], F32) for i in range(2)]
        kvp = [sb(f"kvp{i}", [128, T], F32) for i in range(2)]
        tmp2 = [sb(f"tmp2_{i}", [128, T], F32) for i in range(2)]
        rowb = sb("rowb", [1, 2048], F32)
        srow_g = sb("srow_g", [1, 8], F32)
        rrow = sb("rrow", [1, 8], F32)
        srow2 = sb("srow2", [1, 8], F32)
        effrow = sb("effrow", [1, 8], F32)
        ones_col = sb("ones_col", [128, 1], F32)
        ones_row = sb("ones_row", [1, 128], F32)
        carry = sb("carry", [128, NCH], F32)
        cc_sc = sb("cc_sc", [128, NCH], F32)
        invc = sb("invc", [128, 4], F32)
        sc3 = sb("sc3", [128, 4], F32)
        inv1 = [sb(f"inv1_{i}", [128, 1], F32) for i in range(3)]
        effc = sb("effc", [128, 4], F32)
        redcol = sb("redcol", [128, 48], F32)
        redm = sb("redm", [128, 4], F32)
        g8t = sb("g8t", [8, 8], F32)
        bqc = sb("bqc", [128, NCH], F32)
        bkc = sb("bkc", [128, NCH], F32)
        bvc = sb("bvc", [128, NCH], F32)
        boc = sb("boc", [128, NCH], F32)
        cstc = sb("cstc", [128, 8], F32)
        selt = sb("selt", [128, NCH], F32)
        allg = sb("allg", [128, 128], F32)

        bp = sb("bp", [128, PB_N], BF16)
        ones_sc = sb("ones_sc", [128, T], BF16)

        pb = [ctx.enter_context(nc.psum_tensor(f"pb{i}", [128, T], F32))
              for i in range(8)]

        kb = KB(nc, ctx)

        def bpv(off, n):
            return bp[:, off:off + n]

        def xn_v(c, tl):
            return bpv(XN0 + c * 2048 + tl * T, T)

        def x_c(c):
            return x_t[:, c * T:(c + 1) * T]

        t_ones_sc = kb.dve(lambda e: e.memset(ones_sc[:, :], 1.0))
        t_ones_col = kb.dve(lambda e: e.memset(ones_col[:, :], 1.0))
        t_ones_row = kb.dve(lambda e: e.memset(ones_row[:, :], 1.0))
        t_carry0 = kb.dve(lambda e: e.memset(carry[:, :], 0.0))
        t_srow0 = kb.dve(lambda e: e.memset(srow_g[:, :], 0.0))

        kb.dma("pool", lambda e: e.dma_start(out=cstc[:, :], in_=cst_d), "cld")
        kb.dma("pool", lambda e: e.dma_start(out=bqc[:, :], in_=bq_d), "cld")
        kb.dma("pool", lambda e: e.dma_start(out=bkc[:, :], in_=bk_d), "cld")
        kb.dma("pool", lambda e: e.dma_start(out=bvc[:, :], in_=bv_d), "cld")
        kb.dma("pool", lambda e: e.dma_start(out=boc[:, :], in_=bo_d), "cld")
        tok_cld = kb.dma("pool", lambda e: e.dma_start(out=selt[:, :],
                                                       in_=sel_d), "cld")

        qin_war = [None, None]
        t_red = []

        def scale_load(i):
            sl = i % 2
            return kb.dma("act", lambda e, i=i, sl=sl: e.dma_start(
                out=qin[sl][:, :], in_=wsl_d[i]), f"qld{sl}",
                waits=[qin_war[sl]])

        t_sl = [None] * 48
        t_sl[0] = scale_load(0)
        t_sl[1] = scale_load(1)
        for i in range(48):
            tr = kb.dve(lambda e, i=i, sl=i % 2: e.tensor_reduce(
                redcol[:, i:i + 1], qin[sl][:, :], AX.X, ALU.add,
                apply_absolute_value=True), waits=[t_sl[i]])
            qin_war[i % 2] = tr
            t_red.append(tr)
            if i + 2 < 48:
                t_sl[i + 2] = scale_load(i + 2)
        t_redm = []
        for m in range(3):
            t_redm.append(kb.dve(lambda e, m=m: e.tensor_reduce(
                redm[:, m:m + 1], redcol[:, 16 * m:16 * (m + 1)], AX.X,
                ALU.add), waits=t_red[16 * m:16 * (m + 1)]))
        t_smm = [None] * 3
        for m in range(3):
            t_smm[m] = kb.pe(lambda e, m=m: e.matmul(
                pb[m][0:1, 0:1], redm[:, m:m + 1], ones_col[:, 0:1],
                start=True, stop=True), waits=[t_redm[m], t_ones_col])
        t_srow = None
        for m in range(3):
            t_srow = kb.act(lambda e, m=m: e.copy(
                srow_g[0:1, m:m + 1], pb[m][0:1, 0:1]),
                waits=[t_smm[m], t_srow0])
        t_rin = kb.dma("syncq", lambda e: e.dma_start(out=rin.ap(),
                                                      in_=srow_g[:, :]),
                       "srin", waits=[t_srow])
        t_cc1 = kb.pool(lambda e: e.collective_compute(
            "AllGather", ALU.bypass, replica_groups=[list(range(8))],
            ins=[rin.ap().opt()], outs=[rgout.ap().opt()]), waits=[t_rin],
            sem="cc")
        t_g8 = kb.dma("pool", lambda e: e.dma_start(out=g8t[:, :],
                                                    in_=rgout.ap()),
                      "sg8", waits=[t_cc1])
        t_gsum = kb.pe(lambda e: e.matmul(
            pb[3][0:1, 0:8], ones_col[0:8, 0:1], g8t[0:8, :],
            start=True, stop=True), waits=[t_g8, t_ones_col])
        t_rr = kb.act(lambda e: e.copy(rrow[0:1, 0:8], pb[3][0:1, 0:8]),
                      waits=[t_gsum])
        t_sc2 = kb.dve(lambda e: e.tensor_scalar(
            srow2[:, :], rrow[:, :], 1.0 / NW, 1e-8, op0=ALU.mult,
            op1=ALU.add), waits=[t_rr])
        t_e0 = kb.dve(lambda e: e.tensor_scalar(
            effrow[0:1, 0:1], srow2[0:1, 0:1], 0.5, None, op0=ALU.mult),
            waits=[t_sc2])
        t_e1 = kb.dve(lambda e: e.tensor_tensor(
            effrow[0:1, 1:2], srow2[0:1, 1:2], srow2[0:1, 2:3], ALU.mult),
            waits=[t_sc2])
        t_e1 = kb.dve(lambda e: e.tensor_scalar(
            effrow[0:1, 1:2], effrow[0:1, 1:2], 0.25, None, op0=ALU.mult),
            waits=[t_e1])
        t_bsc = kb.pe(lambda e: e.matmul(
            pb[0][:, 0:3], ones_row[0:1, :], srow2[0:1, 0:3], start=True,
            stop=True), waits=[t_sc2, t_ones_row, t_srow])
        t_beff = kb.pe(lambda e: e.matmul(
            pb[1][:, 0:2], ones_row[0:1, :], effrow[0:1, 0:2], start=True,
            stop=True), waits=[t_e0, t_e1, t_ones_row])
        t_sc3 = kb.act(lambda e: e.copy(sc3[:, 0:3], pb[0][:, 0:3]),
                       waits=[t_bsc])
        t_invc = None
        for m in range(3):
            t_rm = kb.dve(lambda e, m=m: e.reciprocal(inv1[m][:, :],
                                                      sc3[:, m:m + 1]),
                          waits=[t_sc3])
            t_invc = kb.dve(lambda e, m=m: e.tensor_copy(invc[:, m:m + 1],
                                                         inv1[m][:, :]),
                            waits=[t_rm])
        t_effc = kb.act(lambda e: e.copy(effc[:, 0:2], pb[1][:, 0:2]),
                        waits=[t_beff])
        bank_free = {0: t_sc3, 1: t_effc, 2: t_srow, 3: t_rr,
                     4: None, 5: None, 6: None, 7: None}

        q_srcs = (wu_d, wg_d, wd_d)
        q_dsts = (wuq, wgq, wdq)
        qz_tok = {}
        q_jobs = []
        for e2 in range(EHC):
            q_jobs += [(0, 2 * e2), (0, 2 * e2 + 1),
                       (1, 2 * e2), (1, 2 * e2 + 1)]
        q_jobs += [(2, j) for j in range(2 * EHC)]
        q_state = {"i": 0, "qin_war": [t_red[46], t_red[47]],
                   "qb1_war": [None, None], "qb2_war": [None, None],
                   "li": 0, "loads": {}}

        def q_tile_src(m, t_i):
            d = q_srcs[m]
            return d[t_i // 2, (t_i % 2) * 64:(t_i % 2) * 64 + 64, :].rearrange(
                "r (x f) -> (r x) f", x=2)

        def q_tile_dst(m, t_i):
            d = q_dsts[m].ap()
            return d[t_i // 2, (t_i % 2) * 64:(t_i % 2) * 64 + 64, :].rearrange(
                "r (x f) -> (r x) f", x=2)

        def q_load(j):
            st_ = q_state
            m, t_i = q_jobs[j]
            sl = j % 2
            st_["loads"][j] = kb.dma(
                "act", lambda e, m=m, t_i=t_i, sl=sl: e.dma_start(
                    out=qin[sl][:, :], in_=q_tile_src(m, t_i)), f"qld{sl}",
                waits=[st_["qin_war"][sl]])

        def q_emit(n):
            st_ = q_state
            for _ in range(n):
                if st_["i"] >= len(q_jobs):
                    return
                j = st_["i"]
                while st_["li"] <= min(j + 1, len(q_jobs) - 1):
                    q_load(st_["li"])
                    st_["li"] += 1
                m, t_i = q_jobs[j]
                sl = j % 2
                qb1 = bpv(QBo + sl * QTILE, QTILE)
                qb2 = bpv(QB2o + sl * QTILE, QTILE)
                tl_ = st_["loads"].pop(j)
                tb1 = kb.act(lambda e, m=m, sl=sl, qb1=qb1: e.activation(
                    qb1, qin[sl][:, :], AF.Sign,
                    scale=invc[:, m:m + 1], bias=cstc[:, 2:3]),
                    waits=[tl_, t_invc, st_["qb1_war"][sl], tok_cld])
                tb2 = kb.act(lambda e, m=m, sl=sl, qb2=qb2: e.activation(
                    qb2, qin[sl][:, :], AF.Sign,
                    scale=invc[:, m:m + 1], bias=cstc[:, 3:4]),
                    waits=[tl_, t_invc, st_["qb2_war"][sl]])
                st_["qin_war"][sl] = tb2
                tq = kb.dve(lambda e, qb1=qb1, qb2=qb2: e.tensor_tensor(
                    qb1, qb1, qb2, ALU.add), waits=[tb1, tb2])
                st_["qb2_war"][sl] = tq
                tst = kb.dma("syncq", lambda e, m=m, t_i=t_i, qb1=qb1:
                             e.dma_start(out=q_tile_dst(m, t_i), in_=qb1),
                             f"qz{sl}", waits=[tq])
                st_["qb1_war"][sl] = tst
                qz_tok[(m, t_i)] = tst
                st_["i"] += 1

        def q_emit_until(n):
            while q_state["i"] < min(n, len(q_jobs)):
                q_emit(1)

        st = {
            "bank_free": bank_free,
            "x_free": None,
            "rowb_free": None,
            "tmp_free": None,
            "xn_tok": {},
            "xn_readers": [],
            "wA_war": [None, None],
            "wB_war": [None, None],
            "wC_war": [None, None],
            "er_war": [None] * 4,
            "sil_war": [None, None],
            "phik_war": [None] * HT,
            "pqo_war": [None] * 4,
            "kvo_war": [None] * 4,
            "hst_war": [None, None],
            "kvp_war": [None, None],
            "pkv_war": [None] * HT,
            "carry_last": t_carry0,
            "pq_tok": {},
            "kv_tok": {},
            "h_tok": {},
            "g_tok": {},
            "scan_last": None,
        }

        def x_load_tt(tt, waits):
            src = x_d[:, tt * T:(tt + 1) * T].rearrange("(c p) t -> p c t",
                                                        p=128)
            dst = x_t[:, :].rearrange("p (c t) -> p c t", t=T)
            return kb.dma("act", lambda e: e.dma_start(out=dst, in_=src),
                          "xld", waits=waits)

        def emit_ln(half, tl):
            xw = [st["x_free"]] if st["x_free"] else []
            tok_x = x_load_tt(half * HT + tl, xw)
            t_sum = None
            t_sq = [None] * NCH
            t_ssq = [None] * NCH
            for c in range(NCH):
                w0 = [tok_x, st["bank_free"][4]] if c == 0 else [tok_x]
                t_sum = kb.pe(lambda e, c=c: e.matmul(
                    pb[4][0:1, :], ones_col[:, 0:1], x_c(c),
                    start=(c == 0), stop=(c == NCH - 1)),
                    waits=w0 + [t_ones_col])
                t_sq[c] = kb.act(lambda e, c=c: e.square(
                    tmp2[c % 2][:, :], x_c(c)),
                    waits=[tok_x, t_ssq[c - 2] if c >= 2 else st["tmp_free"]])
                wb = [st["bank_free"][5]] if c == 0 else []
                t_ssq[c] = kb.pe(lambda e, c=c: e.matmul(
                    pb[5][0:1, :], ones_col[:, 0:1], tmp2[c % 2][:, :],
                    start=(c == 0), stop=(c == NCH - 1)),
                    waits=wb + [t_sq[c]])
            rw = [st["rowb_free"]] if st["rowb_free"] else []
            t_m = kb.act(lambda e: e.activation(
                rowb[0:1, 0:512], pb[4][0:1, :], AF.Copy,
                scale=cstc[0:1, 0:1]), waits=[t_sum, tok_cld] + rw)
            t_msq = kb.act(lambda e: e.activation(
                rowb[0:1, 512:1024], pb[5][0:1, :], AF.Copy,
                scale=cstc[0:1, 0:1]), waits=[t_ssq[NCH - 1]] + rw)
            t_m2 = kb.dve(lambda e: e.tensor_tensor(
                rowb[0:1, 1024:1536], rowb[0:1, 0:512], rowb[0:1, 0:512],
                ALU.mult), waits=[t_m] + rw)
            t_var = kb.dve(lambda e: e.tensor_tensor(
                rowb[0:1, 512:1024], rowb[0:1, 512:1024],
                rowb[0:1, 1024:1536], ALU.subtract), waits=[t_msq, t_m2])
            t_lnv = kb.act(lambda e: e.activation(
                rowb[0:1, 1536:2048], rowb[0:1, 512:1024], AF.Ln,
                bias=cstc[0:1, 1:2]), waits=[t_var])
            t_r = kb.act(lambda e: e.activation(
                rowb[0:1, 1536:2048], rowb[0:1, 1536:2048], AF.Exp,
                scale=cstc[0:1, 2:3]), waits=[t_lnv])
            t_mr = kb.dve(lambda e: e.tensor_tensor(
                rowb[0:1, 1024:1536], rowb[0:1, 0:512], rowb[0:1, 1536:2048],
                ALU.mult), waits=[t_r, t_m, t_m2])
            t_Rb = kb.pe(lambda e: e.matmul(
                pb[6][:, :], ones_row[0:1, :], rowb[0:1, 1536:2048],
                start=True, stop=True),
                waits=[t_r, t_ones_row, st["bank_free"][6]])
            t_Mb = kb.pe(lambda e: e.matmul(
                pb[7][:, :], ones_row[0:1, :], rowb[0:1, 1024:1536],
                start=True, stop=True), waits=[t_mr, st["bank_free"][7]])
            st["rowb_free"] = t_Mb
            xn_war = list(st["xn_readers"])
            t_t1 = [None] * NCH
            t_xn = [None] * NCH
            for c in range(NCH):
                t_t1[c] = kb.dve(lambda e, c=c: e.tensor_tensor(
                    tmp2[c % 2][:, :], x_c(c), pb[6][:, :], ALU.mult),
                    waits=[t_Rb, tok_x, t_ssq[NCH - 1],
                           t_xn[c - 2] if c >= 2 else None])
                t_xn[c] = kb.dve(lambda e, c=c, tl=tl: e.tensor_tensor(
                    xn_v(c, tl), tmp2[c % 2][:, :], pb[7][:, :],
                    ALU.subtract),
                    waits=[t_Mb, t_t1[c]] + (xn_war if c < 2 else []))
                st["xn_tok"][(half, c, tl)] = t_xn[c]
            st["tmp_free"] = t_xn[NCH - 1]
            st["x_free"] = t_t1[NCH - 1]
            st["bank_free"][4] = t_m
            st["bank_free"][5] = t_msq
            st["bank_free"][6] = t_t1[NCH - 1]
            st["bank_free"][7] = t_xn[NCH - 1]

        def mm_group(bank, wview, rhs_of_c, extra_waits, nk=NCH):
            tokl = None
            for c in range(nk):
                w = list(extra_waits) if c == 0 else []
                tokl = kb.pe(lambda e, c=c: e.matmul(
                    pb[bank][:, :], wview[:, c * 128:(c + 1) * 128],
                    rhs_of_c(c), start=(c == 0), stop=(c == nk - 1)),
                    waits=w, inc=(c == nk - 1))
            return tokl

        def emit_qkv(half, qpace, qpace_kv=None):
            if qpace_kv is None:
                qpace_kv = qpace
            xn_tok = st["xn_tok"]

            def wA_load(wdram, oc, sl):
                return kb.dma("pool", lambda e, oc=oc, sl=sl, wdram=wdram:
                              e.dma_start(out=bpv(WAo + sl * D, D),
                                          in_=wdram[oc]),
                              f"wA{sl}", waits=[st["wA_war"][sl]])

            def phi_epilogue(oc, tl, bank, bias, dst, dst_war_key, war_list):
                t_e = kb.act(lambda e, oc=oc, tl=tl, bank=bank, bias=bias:
                             e.activation(
                                 bpv(ERo + tl * T, T), pb[bank][:, :],
                                 AF.Exp, bias=bias[:, oc:oc + 1]),
                             waits=[war_list[0], st["er_war"][tl], tok_cld])
                t_r2 = kb.act(lambda e, oc=oc, tl=tl, bank=bank, bias=bias:
                              e.activation(
                                  bpv(SILo + (tl % 2) * T, T),
                                  pb[bank][:, :], AF.Relu,
                                  bias=bias[:, oc:oc + 1]),
                              waits=[war_list[0], st["sil_war"][tl % 2]])
                st["bank_free"][bank] = t_r2
                t_phi = kb.dve(lambda e, tl=tl, dst=dst:
                               e.scalar_tensor_tensor(
                                   dst, bpv(ERo + tl * T, T), 1.0,
                                   bpv(SILo + (tl % 2) * T, T),
                                   op0=ALU.min, op1=ALU.add),
                               waits=[t_e, t_r2, dst_war_key])
                st["er_war"][tl] = t_phi
                st["sil_war"][tl % 2] = t_phi
                return t_phi

            t_w = {0: wA_load(wq_d, 0, 0), 1: wA_load(wq_d, 1, 1)}
            for oc in range(NCH):
                sl = oc % 2
                grp_last = None
                for tl in range(HT):
                    bank = tl
                    ew = [t_w[oc], xn_tok[(half, 15, tl)],
                          st["bank_free"][bank]]
                    grp = mm_group(bank, bpv(WAo + sl * D, D),
                                   lambda c, tl=tl: xn_v(c, tl), ew)
                    st["bank_free"][bank] = grp
                    t_phi = phi_epilogue(oc, tl, bank, bqc,
                                         bpv(PQOo + tl * T, T),
                                         st["pqo_war"][tl], [grp])
                    t_pst = kb.dma("syncq", lambda e, oc=oc, half=half, tl=tl:
                                   e.dma_start(
                                       out=phiq_s.ap()[oc][
                                           :, (half * HT + tl) * T:
                                           (half * HT + tl + 1) * T],
                                       in_=bpv(PQOo + tl * T, T)),
                                   f"spq{tl}", waits=[t_phi])
                    st["pqo_war"][tl] = t_pst
                    st["pq_tok"][oc] = t_pst
                    grp_last = grp
                st["wA_war"][sl] = grp_last
                if oc + 2 < NCH:
                    t_w[oc + 2] = wA_load(wq_d, oc + 2, sl)
                if oc >= 2:
                    q_emit(qpace)

            t_wk = {0: wA_load(wk_d, 0, 0)}
            t_wv = {0: wA_load(wv_d, 0, 1)}
            for oc in range(NCH):
                klast = None
                for tl in range(HT):
                    bank = tl
                    ew = [t_wk[oc], xn_tok[(half, 15, tl)],
                          st["bank_free"][bank]]
                    grp = mm_group(bank, bpv(WAo + 0 * D, D),
                                   lambda c, tl=tl: xn_v(c, tl), ew)
                    st["bank_free"][bank] = grp
                    phi_epilogue(oc, tl, bank, bkc, bpv(PHIKo + tl * T, T),
                                 st["phik_war"][tl], [grp])
                    st["phik_war"][tl] = st["er_war"][tl]
                    klast = grp
                st["wA_war"][0] = klast
                if oc + 1 < NCH:
                    t_wk[oc + 1] = wA_load(wk_d, oc + 1, 0)
                vlast = None
                for tl in range(HT):
                    bank = tl
                    ew = [t_wv[oc], st["bank_free"][bank]]
                    grp = mm_group(bank, bpv(WAo + 1 * D, D),
                                   lambda c, tl=tl: xn_v(c, tl), ew)
                    t_pkv = kb.dve(lambda e, oc=oc, tl=tl, bank=bank:
                                   e.scalar_tensor_tensor(
                                       bpv(PKVo + tl * T, T),
                                       pb[bank][:, :], bvc[:, oc:oc + 1],
                                       bpv(PHIKo + tl * T, T),
                                       op0=ALU.add, op1=ALU.mult),
                                   waits=[grp, st["phik_war"][tl],
                                          st["pkv_war"][tl], tok_cld])
                    st["bank_free"][bank] = t_pkv
                    st["phik_war"][tl] = t_pkv
                    vlast = grp
                    ssl = tl % 2
                    t_scan = kb.dve(lambda e, oc=oc, tl=tl, ssl=ssl:
                                    e.tensor_tensor_scan(
                                        kvp[ssl][:, :], ones_sc[:, :],
                                        bpv(PKVo + tl * T, T),
                                        carry[:, oc:oc + 1],
                                        op0=ALU.mult, op1=ALU.add),
                                    waits=[t_pkv, t_ones_sc,
                                           st["carry_last"],
                                           st["kvp_war"][ssl]], force=True)
                    t_carry = kb.dve(lambda e, oc=oc, ssl=ssl: e.tensor_copy(
                        carry[:, oc:oc + 1], kvp[ssl][:, T - 1:T]),
                        waits=[t_scan], force=True)
                    st["carry_last"] = t_carry
                    st["pkv_war"][tl] = t_scan
                    t_kvo = kb.dve(lambda e, ssl=ssl, tl=tl: e.tensor_scalar(
                        bpv(KVOo + tl * T, T), kvp[ssl][:, :], SCALE, None,
                        op0=ALU.mult), waits=[t_scan, st["kvo_war"][tl]])
                    st["kvp_war"][ssl] = t_kvo
                    t_kst = kb.dma("syncq", lambda e, oc=oc, half=half, tl=tl:
                                   e.dma_start(
                                       out=kv_s.ap()[oc][
                                           :, (half * HT + tl) * T:
                                           (half * HT + tl + 1) * T],
                                       in_=bpv(KVOo + tl * T, T)),
                                   f"skv{tl}", waits=[t_kvo])
                    st["kvo_war"][tl] = t_kst
                    st["kv_tok"][oc] = t_kst
                st["wA_war"][1] = vlast
                if oc + 1 < NCH:
                    t_wv[oc + 1] = wA_load(wv_d, oc + 1, 1)
                st["xn_readers"] = [vlast]
                q_emit(qpace_kv)
            st["scan_last"] = st["carry_last"]

        def emit_ug(half, qpace, wqueue="pool"):
            xn_tok = st["xn_tok"]

            qsuf = "a" if wqueue == "act" else ""

            def wB_load(eh):
                sl = eh % 2
                return kb.dma(wqueue, lambda e, eh=eh, sl=sl: e.dma_start(
                    out=bpv(WBo + sl * D, D), in_=wuq.ap()[eh]),
                    f"wB{sl}{qsuf}",
                    waits=[st["wB_war"][sl], qz_tok.get((0, 2 * eh)),
                           qz_tok.get((0, 2 * eh + 1))])

            def wC_load(eh):
                sl = eh % 2
                return kb.dma(wqueue, lambda e, eh=eh, sl=sl: e.dma_start(
                    out=bpv(WCo + sl * D, D), in_=wgq.ap()[eh]),
                    f"wC{sl}{qsuf}",
                    waits=[st["wC_war"][sl], qz_tok.get((1, 2 * eh)),
                           qz_tok.get((1, 2 * eh + 1))])

            t_wu = {0: wB_load(0), 1: wB_load(1)}
            t_wg = {0: wC_load(0), 1: wC_load(1)}
            for eh in range(EHC):
                sl = eh % 2
                hsl = eh % 2
                up_last = gate_last = None
                t_h = None
                for tl in range(HT):
                    ub = tl % 2
                    gb = 2 + tl % 2
                    ew = [t_wu[eh], xn_tok[(half, 15, tl)],
                          st["bank_free"][ub]]
                    up = mm_group(ub, bpv(WBo + sl * D, D),
                                  lambda c, tl=tl: xn_v(c, tl), ew)
                    t_sil = kb.act(lambda e, ub=ub, tl=tl: e.activation(
                        bpv(SILo + (tl % 2) * T, T), pb[ub][:, :], AF.Silu,
                        scale=effc[:, 0:1]),
                        waits=[up, t_effc, st["sil_war"][tl % 2]])
                    st["bank_free"][ub] = t_sil
                    ewg = [t_wg[eh], st["bank_free"][gb]]
                    gate = mm_group(gb, bpv(WCo + sl * D, D),
                                    lambda c, tl=tl: xn_v(c, tl), ewg)
                    t_h = kb.dve(lambda e, gb=gb, tl=tl, hsl=hsl:
                                 e.tensor_tensor(
                                     bpv(HSTo + hsl * 2048 + tl * T, T),
                                     bpv(SILo + (tl % 2) * T, T),
                                     pb[gb][:, :], ALU.mult),
                                 waits=[t_sil, gate, st["hst_war"][hsl]])
                    st["sil_war"][tl % 2] = t_h
                    st["bank_free"][gb] = t_h
                    up_last, gate_last = up, gate
                t_hst = kb.dma("syncq", lambda e, eh=eh, half=half, hsl=hsl:
                               e.dma_start(
                                   out=h_s.ap()[eh][:, half * HT * T:
                                                    (half + 1) * HT * T],
                                   in_=bpv(HSTo + hsl * 2048, 2048)),
                               f"sh{hsl}", waits=[t_h])
                st["hst_war"][hsl] = t_hst
                st["h_tok"][(eh, half)] = t_hst
                st["wB_war"][sl] = up_last
                st["wC_war"][sl] = gate_last
                if eh + 2 < EHC:
                    t_wu[eh + 2] = wB_load(eh + 2)
                    t_wg[eh + 2] = wC_load(eh + 2)
                st["xn_readers"] = [gate_last]
                q_emit(qpace)
                yield eh

        for tl in range(HT):
            emit_ln(0, tl)
        emit_qkv(0, qpace=3, qpace_kv=4)
        for _ in emit_ug(0, qpace=3):
            pass

        for tl in range(HT):
            emit_ln(1, tl)
        emit_qkv(1, qpace=0)

        t_tin = None
        for c in range(NCH):
            t_tin = kb.dma("syncq", lambda e, c=c: e.dma_start(
                out=tin.ap()[c:c + 1, :].rearrange("a b -> b a"),
                in_=carry[:, c:c + 1]), "stin", waits=[st["scan_last"]])
        t_cc2 = kb.pool(lambda e: e.collective_compute(
            "AllGather", ALU.bypass, replica_groups=[list(range(8))],
            ins=[tin.ap().opt()], outs=[tout.ap().opt()]),
            waits=[t_tin, t_cc1], sem="cc")
        t_allg = kb.dma("pool", lambda e: e.dma_start(
            out=allg[:, :], in_=tout.ap()), "sg8", waits=[t_cc2])

        t_ccsc = None
        for eh in emit_ug(1, qpace=2, wqueue="act"):
            if eh == 32:
                t_selmm = kb.pe(lambda e: e.matmul(
                    pb[5][:, 0:NCH], allg[:, :], selt[:, :], start=True,
                    stop=True), waits=[t_allg, tok_cld, st["bank_free"][5]])
                t_ccsc = kb.act(lambda e: e.copy(cc_sc[:, :],
                                                 pb[5][:, 0:NCH]),
                                waits=[t_selmm])
                st["bank_free"][5] = t_ccsc

        q_emit_until(len(q_jobs))

        region_free = ([st["wB_war"][0], st["wB_war"][1], st["wC_war"][0],
                        st["wC_war"][1], st["hst_war"][0], st["hst_war"][1],
                        q_state["qb1_war"][0], q_state["qb1_war"][1],
                        q_state["qb2_war"][0], q_state["qb2_war"][1],
                        st["sil_war"][0], st["sil_war"][1]]
                       + list(st["pkv_war"])
                       + list(st["er_war"]) + list(st["phik_war"])
                       + list(st["pqo_war"]) + list(st["kvo_war"]))
        region_free = [t for t in region_free if t is not None]
        xn_free = list(st["xn_readers"])

        def wd_tile(ocl, j):
            return bpv(WDo + (ocl * 8 + j) * QTILE, QTILE)

        dn = {
            "htt_war": [None] * HSLOT,
            "wd_war": {},
            "gdo_war": [None] * 4,
            "ctx_tok": {},
            "ctx_kv_war": [None, None],
            "cstg_war": [None, None],
            "fill_tok": {},
        }

        def emit_ctx(c, extra_waits=()):
            co = ctx_off(c)
            t_kv = kb.dma("act", lambda e, c=c, co=co: e.dma_start(
                out=bpv(co, 4096), in_=kv_s.ap()[c]), f"ckv{c % 2}",
                waits=[st["kv_tok"][c], dn["ctx_kv_war"][c % 2]]
                + list(extra_waits))
            t_ctx = None
            for hh in range(4):
                qsl = hh % 2
                t_pq = kb.dma("syncq", lambda e, c=c, hh=hh, qsl=qsl:
                              e.dma_start(
                                  out=bpv(CSTGo + qsl * 1024, 1024),
                                  in_=phiq_s.ap()[c][:, hh * 1024:
                                                     (hh + 1) * 1024]),
                              f"cpq{qsl}", waits=[st["pq_tok"][c],
                                                  dn["cstg_war"][qsl]])
                t_ctx = kb.dve(lambda e, c=c, co=co, hh=hh, qsl=qsl:
                               e.scalar_tensor_tensor(
                                   bpv(co + hh * 1024, 1024),
                                   bpv(co + hh * 1024, 1024),
                                   cc_sc[:, c:c + 1],
                                   bpv(CSTGo + qsl * 1024, 1024),
                                   op0=ALU.add, op1=ALU.mult),
                               waits=[t_kv, t_pq, t_ccsc])
                dn["cstg_war"][qsl] = t_ctx
            dn["ctx_kv_war"][c % 2] = t_ctx
            dn["ctx_tok"][c] = t_ctx

        ctx_queue = list(range(8, NCH))
        h_queues = ["act", "pool"]

        def h_load(ec, td, war):
            slot = ec % HSLOT
            return kb.dma(
                h_queues[ec % 2],
                lambda e, ec=ec, td=td, slot=slot: e.dma_start(
                    out=bpv(HTTo + slot * TD, TD),
                    in_=h_s.ap()[ec][:, td * TD:(td + 1) * TD]),
                f"ht{slot}", waits=[st["h_tok"][(ec, td // 8)], war])

        for oq in range(4):
            fill_prev = {}
            nf = 0
            for j in range(8):
                for ocl in range(4):
                    oc = oq * 4 + ocl
                    k = oc * 8 + j
                    fsl = nf % 8
                    w = [qz_tok[(2, k)], fill_prev.get(fsl),
                         dn["wd_war"].get((ocl, j))]
                    if oq == 0 and nf == 0:
                        w += region_free + xn_free
                    tf = kb.dma("syncq",
                                lambda e, ocl=ocl, j=j, k=k: e.dma_start(
                                    out=wd_tile(ocl, j),
                                    in_=q_tile_dst(2, k)),
                                f"wdf{fsl}", waits=w)
                    dn["fill_tok"][(oq, ocl, j)] = tf
                    fill_prev[fsl] = tf
                    nf += 1
            for td in range(NTD):
                t_hld = {}
                for ec in range(HSLOT):
                    t_hld[ec] = h_load(ec, td, dn["htt_war"][ec % HSLOT])
                for ec in range(EHC):
                    slot = ec % HSLOT
                    j, e8 = ec // 8, ec % 8
                    last8 = None
                    for ocl in range(4):
                        bank = (td % 2) * 4 + ocl
                        w = [t_hld[ec]]
                        if ec == 0:
                            w += [dn["fill_tok"][(oq, ocl, 0)],
                                  st["bank_free"][bank]]
                        elif e8 == 0:
                            w.append(dn["fill_tok"][(oq, ocl, j)])
                        last8 = kb.pe(
                            lambda e, ocl=ocl, j=j, e8=e8, slot=slot,
                            bank=bank, ec=ec: e.matmul(
                                pb[bank][:, 0:TD],
                                wd_tile(ocl, j)[:, e8 * 128:(e8 + 1) * 128],
                                bpv(HTTo + slot * TD, TD),
                                start=(ec == 0), stop=(ec == EHC - 1)),
                            waits=w, inc=(ocl == 3))
                    for ocl in range(4):
                        dn["wd_war"][(ocl, j)] = last8
                    dn["htt_war"][slot] = last8
                    if ec + HSLOT < EHC:
                        t_hld[ec + HSLOT] = h_load(ec + HSLOT, td, last8)
                for ocl in range(4):
                    oc = oq * 4 + ocl
                    bank = (td % 2) * 4 + ocl
                    t_g = kb.act(lambda e, bank=bank, ocl=ocl: e.activation(
                        bpv(GDOo + ocl * TD, TD), pb[bank][:, 0:TD],
                        AF.Silu, scale=effc[:, 1:2]),
                        waits=[dn["wd_war"][(ocl, 7)], t_effc,
                               dn["gdo_war"][ocl]])
                    st["bank_free"][bank] = t_g
                    t_gst = kb.dma("syncq", lambda e, oc=oc, td=td, ocl=ocl:
                                   e.dma_start(
                                       out=g_s.ap()[oc][:, td * TD:
                                                        (td + 1) * TD],
                                       in_=bpv(GDOo + ocl * TD, TD)),
                                   f"sgd{ocl}", waits=[t_g])
                    dn["gdo_war"][ocl] = t_gst
                    st["g_tok"][oc] = t_gst
                if ctx_queue and td % 3 == 2:
                    c = ctx_queue.pop(0)
                    emit_ctx(c, extra_waits=(xn_free if c == 8 else ()))
        while ctx_queue:
            c = ctx_queue.pop(0)
            emit_ctx(c, extra_waits=(xn_free if c == 8 else ()))

        down_done = [dn["wd_war"][(ocl, 7)] for ocl in range(4)]
        htt_done = [t for t in dn["htt_war"] if t is not None]
        for c in range(8):
            emit_ctx(c, extra_waits=(down_done if c < 2 else ()))

        fin = {
            "wo_war": None,
            "x_war": [None, None],
            "gin_war": None,
            "sgt_war": [None, None],
            "y_tok": [None, None],
        }
        for oc in range(NCH):
            sl = oc % 2
            t_wo = kb.dma("pool", lambda e, oc=oc: e.dma_start(
                out=bpv(WOo, D), in_=wo_d[oc]), "wo0",
                waits=[fin["wo_war"]] + (htt_done if oc == 0 else []))
            t_x = kb.dma("act", lambda e, oc=oc, sl=sl: e.dma_start(
                out=x_t[:, sl * 4096:(sl + 1) * 4096],
                in_=x_d[oc * 128:(oc + 1) * 128, :]), f"fx{sl}",
                waits=[fin["y_tok"][sl], fin["x_war"][sl]]
                + ([st["x_free"], st["tmp_free"]] if oc < 2 else []))
            t_gin = kb.dma("pool", lambda e, oc=oc: e.dma_start(
                out=bpv(GINo, 4096), in_=g_s.ap()[oc]), "fg",
                waits=[st["g_tok"][oc], fin["gin_war"]]
                + (htt_done if oc == 0 else []))
            t_sgt = None
            t_out = [None] * NTT
            c_order = list(range(8, NCH)) + list(range(8))
            for tt in range(NTT):
                bank = 4 + tt % 4
                grp = None
                for i, c in enumerate(c_order):
                    w = [t_wo, st["bank_free"][bank]] if i == 0 else []
                    if oc == 0:
                        w.append(dn["ctx_tok"][c])
                    grp = kb.pe(lambda e, c=c, i=i, bank=bank, tt=tt:
                                e.matmul(
                                    pb[bank][:, :],
                                    bpv(WOo + c * 128, 128),
                                    bpv(ctx_off(c) + tt * T, T),
                                    start=(i == 0), stop=(i == NCH - 1)),
                                waits=w, inc=(i == NCH - 1))
                ssl = tt % 2
                sgw = [grp, t_gin, fin["sgt_war"][ssl]]
                if oc == 0 and tt == 0:
                    sgw += [t for t in dn["gdo_war"] if t is not None]
                t_sgt = kb.dve(lambda e, oc=oc, tt=tt, ssl=ssl, bank=bank:
                               e.scalar_tensor_tensor(
                                   bpv(SGTo + ssl * T, T), pb[bank][:, :],
                                   boc[:, oc:oc + 1],
                                   bpv(GINo + tt * T, T),
                                   op0=ALU.add, op1=ALU.mult),
                               waits=sgw)
                st["bank_free"][bank] = t_sgt
                t_out[tt] = kb.dve(lambda e, tt=tt, sl=sl, ssl=ssl:
                                   e.tensor_tensor(
                                       x_t[:, sl * 4096 + tt * T:
                                           sl * 4096 + (tt + 1) * T],
                                       x_t[:, sl * 4096 + tt * T:
                                           sl * 4096 + (tt + 1) * T],
                                       bpv(SGTo + ssl * T, T), ALU.add),
                                   waits=[t_sgt, t_x])
                fin["sgt_war"][ssl] = t_out[tt]
            fin["wo_war"] = grp
            fin["gin_war"] = t_sgt
            fin["x_war"][sl] = t_out[NTT - 1]
            t_y = kb.dma("syncq", lambda e, oc=oc, sl=sl: e.dma_start(
                out=y_d[oc], in_=x_t[:, sl * 4096:(sl + 1) * 4096]),
                f"syo{sl}", waits=[t_out[NTT - 1]])
            fin["y_tok"][sl] = t_y

        kb.cnt["cc"] += 1

        def _final_sync(e):
            e.wait_ge(kb.sems["syo0"], kb.cnt["syo0"])
            e.wait_ge(kb.sems["syo1"], kb.cnt["syo1"])
            e.nop().then_inc(kb.sems["cc"], 1)
        kb.op("syncq", _final_sync)

        def _cleanup(e):
            for sname in kb.sems:
                e.wait_ge(kb.sems[sname], kb.cnt[sname])
            if _CLEANUP != "waits_only":
                for sname in kb.sems:
                    e.sem_clear(kb.sems[sname])
        if _CLEANUP:
            kb.op("pool", _cleanup)

        if _ENTRY_CLEAR:
            nums = sorted(h.num for h in kb.sems.values())
            nc.all_engine_barrier()
            nc.gpsimd.dma_reset(range(nums[0], nums[-1] + 1))
            nc.gpsimd.sem_clear(range(nums[0], nums[-1] + 1))
            nc.all_engine_barrier()
        with nc.Block() as block:
            @block.tensor
            def _(eng):
                for f in kb.plan["pe"]:
                    f(eng)

            @block.scalar
            def _(eng):
                for f in kb.plan["act"]:
                    f(eng)

            @block.vector
            def _(eng):
                for f in kb.plan["dve"]:
                    f(eng)

            @block.gpsimd
            def _(eng):
                for f in kb.plan["pool"]:
                    f(eng)

            @block.sync
            def _(eng):
                for f in kb.plan["syncq"]:
                    f(eng)

    return nc


def _prep_shared(Wq, Wk, Wv, Wo, Wu, Wg, Wd, bq, bk, bv, bo):
    def attn_stream(W):
        WT = np.ascontiguousarray(np.asarray(W, np.float32).T)
        return np.ascontiguousarray(
            WT.reshape(16, 128, 16, 128).transpose(2, 1, 0, 3)
        ).reshape(16, 128, 2048)

    def upgate_stream(W):
        WT = np.ascontiguousarray(np.asarray(W, np.float32).T)
        return np.ascontiguousarray(
            WT.reshape(16, 128, 64, 128).transpose(2, 1, 0, 3)
        ).reshape(64, 128, 2048)

    def down_stream(W):
        WT = np.ascontiguousarray(np.asarray(W, np.float32).T)
        A = WT.reshape(64, 128, 16, 128).transpose(2, 0, 1, 3)
        A = A.reshape(16, 8, 8, 128, 128).transpose(0, 1, 3, 2, 4)
        return np.ascontiguousarray(A).reshape(64, 128, 2048)

    def bcol(b):
        return np.ascontiguousarray(np.asarray(b, np.float32).reshape(16, 128).T)

    d = {
        "wq": attn_stream(Wq), "wk": attn_stream(Wk), "wv": attn_stream(Wv),
        "wo": attn_stream(Wo),
        "wu": upgate_stream(Wu), "wg": upgate_stream(Wg), "wd": down_stream(Wd),
        "bq": bcol(bq), "bk": bcol(bk), "bv": bcol(bv), "bo": bcol(bo),
        "cst": np.tile(np.array([[1.0 / 2048, 1e-5, -0.5, 0.5, 0, 0, 0, 0]],
                                np.float32), (128, 1)),
    }
    return d


def kernel(L0_attn_x, L0_attn_Wq, L0_attn_bq, L0_attn_Wk, L0_attn_bk,
           L0_attn_Wv, L0_attn_bv, L0_attn_Wo, L0_attn_bo,
           L0_ffn_W_up, L0_ffn_W_gate, L0_ffn_W_down):
    x = np.asarray(L0_attn_x, np.float32)
    shared = _prep_shared(L0_attn_Wq, L0_attn_Wk, L0_attn_Wv, L0_attn_Wo,
                          L0_ffn_W_up, L0_ffn_W_gate, L0_ffn_W_down,
                          L0_attn_bq, L0_attn_bk, L0_attn_bv, L0_attn_bo)

    wu_flat = shared["wu"].reshape(8192, 2048)
    wg_flat = shared["wg"].reshape(8192, 2048)
    wd_flat = shared["wd"].reshape(8192, 2048)

    in_maps = []
    for c in range(8):
        b, h = c // 2, c % 2
        x_fm = np.ascontiguousarray(x[b, h * 4096:(h + 1) * 4096, :].T)
        sel = np.zeros((128, 16), np.float32)
        if h == 1:
            r = c - 1
            for ch in range(16):
                sel[16 * r + ch, ch] = SCALE
        wsl = np.ascontiguousarray(np.stack([
            wu_flat[c * 1024:(c + 1) * 1024],
            wg_flat[c * 1024:(c + 1) * 1024],
            wd_flat[c * 1024:(c + 1) * 1024]])).reshape(48, 128, 1024)
        m = dict(shared)
        m["x"] = x_fm
        m["sel"] = sel
        m["wsl"] = wsl
        in_maps.append(m)

    nc = build_nc()
    res = run_bass_kernel_spmd(nc, in_maps, core_ids=list(range(8)))

    out = np.empty((4, 8192, 2048), np.float32)
    for c in range(8):
        b, h = c // 2, c % 2
        y = np.asarray(res.results[c]["y"])
        out_fm = y.reshape(2048, 4096)
        out[b, h * 4096:(h + 1) * 4096, :] = out_fm.T
    return out
